# version 16
# baseline (speedup 1.0000x reference)
import numpy as np
import ml_dtypes
from contextlib import ExitStack
import concourse.bass as bass
import concourse.mybir as mybir
from concourse.bass_utils import run_bass_kernel_spmd

F32 = mybir.dt.float32
BF16 = mybir.dt.bfloat16
I32 = mybir.dt.int32
AF = mybir.ActivationFunctionType
ALU = mybir.AluOpType
AX = mybir.AxisListType

NCORES = 8
SEQS = 4
S = 2048
D = 1024
HID = 3072
NH = 16
HD = 64
NCH = S // 128
KC = D // 128
SLAB = 1024
NSLAB = HID // SLAB
EPS = 1e-5
SFIN = 0.5 / (127.0 * 127.0)
np32 = np.float32

PHASES = "ABC"
DBG_SLABS = None
DBG_TAPS = False


class Em:
    ENGS = ("tensor", "vector", "scalar", "gpsimd", "sync")

    def __init__(self):
        self.ins = {e: [] for e in self.ENGS}
        self.res = {}
        self.streams = {}
        self.dcnt = {}

    def stream(self, name, eng):
        self.streams[name] = eng
        self.dcnt[name] = 0

    def _rdeps(self, key, eng):
        r = self.res.get(key)
        deps = set()
        if r:
            if r["w"] is not None:
                deps.add(r["w"])
            if key.startswith("PSB"):
                for rv in r["r"]:
                    if rv[0] == "d" or rv[1] != eng:
                        deps.add(rv)
        return deps

    def _wdeps(self, key):
        r = self.res.get(key)
        deps = set()
        if r:
            if r["w"] is not None:
                deps.add(r["w"])
            deps.update(r["r"])
        return deps

    def op(self, eng, fn, reads=(), writes=(), war=(), deps_extra=()):
        deps = set(deps_extra)
        for k in reads:
            deps |= self._rdeps(k, eng)
        for k in writes:
            deps |= self._wdeps(k)
        for k in war:
            deps |= self._wdeps(k)
        idx = len(self.ins[eng])
        ev = ("e", eng, idx)
        self.ins[eng].append(dict(fn=fn, deps=deps, inc=False, stream=None, val=None))
        for k in reads:
            self.res.setdefault(k, dict(w=None, r=[]))["r"].append(ev)
        for k in writes:
            self.res[k] = dict(w=ev, r=[])
        return ev

    def dma(self, stream, fn, reads=(), writes=(), war=()):
        eng = self.streams[stream]
        deps = set()
        for k in reads:
            deps |= self._rdeps(k, eng)
        for k in writes:
            deps |= self._wdeps(k)
        for k in war:
            deps |= self._wdeps(k)
        if self.dcnt[stream] >= 1:
            deps.add(("d", stream, self.dcnt[stream]))
        self.dcnt[stream] += 1
        ev = ("d", stream, self.dcnt[stream])
        self.ins[eng].append(dict(fn=fn, deps=deps, inc=True, stream=stream, val=None))
        for k in reads:
            self.res.setdefault(k, dict(w=None, r=[]))["r"].append(ev)
        for k in writes:
            self.res[k] = dict(w=ev, r=[])
        return ev

    def group(self, eng, items, writes, war=()):
        deps0 = set()
        for k in writes:
            deps0 |= self._wdeps(k)
        for k in war:
            deps0 |= self._wdeps(k)
        last = None
        for i, (fn, reads) in enumerate(items):
            deps = set(deps0) if i == 0 else set()
            for k in reads:
                deps |= self._rdeps(k, eng)
            idx = len(self.ins[eng])
            ev = ("e", eng, idx)
            self.ins[eng].append(dict(fn=fn, deps=deps, inc=False, stream=None, val=None))
            for k in reads:
                self.res.setdefault(k, dict(w=None, r=[]))["r"].append(ev)
            last = ev
        for k in writes:
            self.res[k] = dict(w=last, r=[])
        return last

    def check_deadlock(self):
        ptr = {e: 0 for e in self.ENGS}
        done_val = {}
        for e in self.ENGS:
            done_val[("e", e)] = 0
        for st in self.streams:
            done_val[("d", st)] = 0
        vals = {e: [] for e in self.ENGS}
        cnt = {e: 0 for e in self.ENGS}
        dcnt2 = {st: 0 for st in self.streams}
        for e in self.ENGS:
            for inst in self.ins[e]:
                if inst["stream"] is not None:
                    dcnt2[inst["stream"]] += 1
                    vals[e].append(("d", inst["stream"], dcnt2[inst["stream"]] * 16))
                elif inst["inc"]:
                    cnt[e] += 1
                    vals[e].append(("e", e, cnt[e]))
                else:
                    cnt_e = cnt[e]
                    vals[e].append(None)
        def dep_ok(dep):
            kind, name, v = dep
            if kind == "e":
                tgt = self.ins[name][v]
                return done_val[("e", name)] >= (tgt["val"] or 0)
            return done_val[("d", name)] >= 16 * v
        for e in self.ENGS:
            c2 = 0
            for inst in self.ins[e]:
                if inst["stream"] is None and inst["inc"]:
                    c2 += 1
                    inst["val"] = c2
        progress = True
        while progress:
            progress = False
            for e in self.ENGS:
                while ptr[e] < len(self.ins[e]):
                    inst = self.ins[e][ptr[e]]
                    if all(dep_ok(d) for d in inst["deps"]):
                        mark = vals[e][ptr[e]]
                        if mark is not None:
                            if mark[0] == "d":
                                done_val[("d", mark[1])] = mark[2]
                            else:
                                done_val[("e", e)] = mark[2]
                        elif inst["inc"]:
                            pass
                        ptr[e] += 1
                        progress = True
                    else:
                        break
        stuck = {e: ptr[e] for e in self.ENGS if ptr[e] < len(self.ins[e])}
        if stuck:
            print("DEADLOCK DETECTED; engine heads:")
            for e, p in stuck.items():
                inst = self.ins[e][p]
                bad = [d for d in inst["deps"] if not dep_ok(d)]
                print(f"  {e}[{p}]: unmet deps {bad[:4]}")
            raise AssertionError("deadlock in emitted program")

    def finalize_and_emit(self, nc, es):
        for eng in self.ENGS:
            for inst in self.ins[eng]:
                best = {}
                for dep in inst["deps"]:
                    kind, name, v = dep
                    kk = (kind, name)
                    if kk not in best or v > best[kk][2]:
                        best[kk] = dep
                inst["deps"] = sorted(best.values(), key=lambda d: (d[0], str(d[1]), d[2]))
        for eng in self.ENGS:
            for inst in self.ins[eng]:
                for dep in inst["deps"]:
                    if dep[0] == "e":
                        self.ins[dep[1]][dep[2]]["inc"] = True
        for eng in self.ENGS:
            cnt = 0
            for inst in self.ins[eng]:
                if inst["stream"] is None and inst["inc"]:
                    cnt += 1
                    inst["val"] = cnt
        self.check_deadlock()
        semh = {}
        for eng in self.ENGS:
            semh[("e", eng)] = es.enter_context(nc.semaphore(f"se_{eng}"))
        for st in self.streams:
            semh[("d", st)] = es.enter_context(nc.semaphore(f"sd_{st}"))

        def resolve(dep):
            kind, name, v = dep
            if kind == "e":
                return semh[("e", name)], self.ins[name][v]["val"]
            return semh[("d", name)], 16 * v

        blk = es.enter_context(nc.Block())
        em = self

        def body(engobj, eng):
            waited = {}
            for inst in em.ins[eng]:
                for dep in inst["deps"]:
                    sem, val = resolve(dep)
                    key = id(sem)
                    if waited.get(key, 0) < val:
                        engobj.wait_ge(sem, val)
                        waited[key] = val
                bi = inst["fn"](engobj)
                if inst["stream"] is not None:
                    bi.then_inc(semh[("d", inst["stream"])], 16)
                elif inst["inc"]:
                    bi.then_inc(semh[("e", eng)], 1)

        for name, starter in (("tensor", blk.tensor), ("vector", blk.vector),
                              ("scalar", blk.scalar), ("gpsimd", blk.gpsimd),
                              ("sync", blk.sync)):
            if self.ins[name]:
                starter(lambda engobj, name=name: body(engobj, name))


def build_program():
    nc = bass.Bass("TRN2", target_bir_lowering=False)

    x_d = nc.dram_tensor("x", [SEQS, S, D], F32, kind="ExternalInput").ap()
    o_d = nc.dram_tensor("o", [SEQS, S, D], F32, kind="ExternalOutput").ap()
    if DBG_TAPS:
        dbg_xn = nc.dram_tensor("dbg_xn", [128, KC * S], F32, kind="ExternalOutput").ap()
        dbg_sg = nc.dram_tensor("dbg_sg", [128, 4 * S], F32, kind="ExternalOutput").ap()
        dbg_h = nc.dram_tensor("dbg_h", [128, 8 * S], F32, kind="ExternalOutput").ap()
        dbg_acc = nc.dram_tensor("dbg_acc", [128, NCH * D], F32, kind="ExternalOutput").ap()
        dbg_wa = nc.dram_tensor("dbg_wa", [128, KC * SLAB], F32, kind="ExternalOutput").ap()
        dbg_ps = nc.dram_tensor("dbg_ps", [128, 512], F32, kind="ExternalOutput").ap()
    w_dram = dict(
        wq=nc.dram_tensor("wq", [D, D], F32, kind="ExternalInput").ap(),
        wv=nc.dram_tensor("wv", [D, D], F32, kind="ExternalInput").ap(),
        wo=nc.dram_tensor("wo", [D, D], F32, kind="ExternalInput").ap(),
        wgt=nc.dram_tensor("wgt", [D, D], F32, kind="ExternalInput").ap(),
        wgate=nc.dram_tensor("wgate", [HID, D], I32, kind="ExternalInput").ap(),
        wup=nc.dram_tensor("wup", [HID, D], I32, kind="ExternalInput").ap(),
        wdown=nc.dram_tensor("wdown", [D, HID], I32, kind="ExternalInput").ap(),
    )
    gam_d = nc.dram_tensor("gam", [1, D], F32, kind="ExternalInput").ap()
    bet_d = nc.dram_tensor("bet", [1, D], F32, kind="ExternalInput").ap()
    ab_r_d = nc.dram_tensor("ab_r", [1, NH], F32, kind="ExternalInput").ap()
    ab_c_d = nc.dram_tensor("ab_c", [NH, 1], F32, kind="ExternalInput").ap()
    grid_d = nc.dram_tensor("gridc", [128, 128], F32, kind="ExternalInput").ap()
    mask_d = nc.dram_tensor("maskc", [128, 128], F32, kind="ExternalInput").ap()
    ones_d = nc.dram_tensor("onesc", [1, 128], F32, kind="ExternalInput").ap()
    shead_d = nc.dram_tensor("sheadc", [NH, D], F32, kind="ExternalInput").ap()
    ident_d = nc.dram_tensor("identc", [128, 128], BF16, kind="ExternalInput").ap()

    nat = {
        "wgate": nc.dram_tensor("nat_g", [HID, D], BF16, kind="Internal").ap(),
        "wup": nc.dram_tensor("nat_u", [HID, D], BF16, kind="Internal").ap(),
        "wdown": nc.dram_tensor("nat_d", [D, HID], BF16, kind="Internal").ap(),
    }
    for w in ("wq", "wv", "wo", "wgt"):
        nat[w] = nc.dram_tensor(f"nat_{w}", [D, D], BF16, kind="Internal").ap()
    scr_g = nc.dram_tensor("scr_g", [128, NSLAB, KC, SLAB], BF16, kind="Internal").ap()
    scr_u = nc.dram_tensor("scr_u", [128, NSLAB, KC, SLAB], BF16, kind="Internal").ap()
    scr_d = nc.dram_tensor("scr_d", [128, NSLAB, SLAB // 128, D], BF16, kind="Internal").ap()
    scr_a = {w: nc.dram_tensor(f"scr_{w}", [128, KC, D], BF16, kind="Internal").ap()
             for w in ("wq", "wv", "wo", "wgt")}

    es = ExitStack()
    sb = lambda n, sh, dt: es.enter_context(nc.sbuf_tensor(n, sh, dt))
    pst = lambda n, sh, dt: es.enter_context(nc.psum_tensor(n, sh, dt))

    X = sb("X", [128, NCH * D], BF16)
    XN = sb("XN", [128, KC * S], BF16)
    WA = sb("WA", [128, KC * SLAB], BF16)
    WB = sb("WB", [128, KC * SLAB], BF16)
    SG = sb("SG", [128, 4 * S], BF16)
    H = sb("H", [128, 8 * S], BF16)
    ACC = sb("ACC", [128, NCH * D], BF16)

    LBUF = sb("LBUF", [128, NH * 128], F32)
    LAMV = sb("LAMV", [128, D], F32)
    GREP = sb("GREP", [128, D], BF16)
    BREP = sb("BREP", [128, D], BF16)
    LLREP = sb("LLREP", [128, NH], F32)
    LAMR = sb("LAMR", [128, NH], F32)
    STATS = sb("STATS", [128, 128], F32)
    TMPTM = sb("TMPTM", [128, 2 * D], BF16)
    CROW = sb("CROW", [1, D], F32)
    TSTB = sb("TSTB", [128, 2 * D], BF16)
    CTMP = TSTB[0:1, :].bitcast(F32)
    IDN = sb("IDN", [128, 128], BF16)
    SM1 = sb("SM1", [1, NH], F32)
    SM2 = sb("SM2", [NH, 1], F32)
    SM3 = sb("SM3", [1, NH], F32)
    SM4 = sb("SM4", [NH, 1], F32)
    LVROW = CROW[0:1, :]

    PSB = [pst(f"PSB{i}", [128, 512], F32) for i in range(7)]
    PSB7 = pst("PSB7b", [128, 1024], BF16)

    ACCF = ACC[:].bitcast(F32)
    GRIDC = ACCF[:, 0:128]
    MASKC = ACCF[:, 128:256]
    ONESC = ACCF[0:1, 256:384]
    SHEADC = ACCF[0:NH, 384:1408]
    GAMR = ACCF[0:1, 1408:2432]
    BETR = ACCF[0:1, 2432:3456]

    HF = H[:].bitcast(F32)
    VSC = [HF[:, 1024 * i:1024 * (i + 1)] for i in range(2)]
    STSB = [HF[:, 1024 * (2 + i):1024 * (3 + i)] for i in range(2)]
    QC = [H[:, 8192 + 1024 * i:8192 + 1024 * (i + 1)] for i in range(2)]
    YC = [H[:, 10240 + 1024 * i:10240 + 1024 * (i + 1)] for i in range(2)]
    SGC = [H[:, 12288 + 1024 * i:12288 + 1024 * (i + 1)] for i in range(3)]
    WOT = ACC[:, 0:8192]
    TYC = [ACC[:, 8192 + 1024 * i:8192 + 1024 * (i + 1)] for i in range(2)] \
        + [ACC[:, 14336:15360]]
    Y2C = [ACC[:, 10240 + 1024 * i:10240 + 1024 * (i + 1)] for i in range(2)]
    Y2FM = [ACC[:, 12288 + 1024 * i:12288 + 1024 * (i + 1)] for i in range(2)]

    em = Em()
    em.stream("pc", "gpsimd")
    em.stream("cl", "gpsimd")
    for q in range(4):
        em.stream(f"io{q}", "gpsimd")
    em.stream("cr", "gpsimd")
    em.stream("ns", "sync")
    em.stream("wl", "sync")
    for t_ in ("g", "u", "d", "a"):
        em.stream(f"tl{t_}", "sync")
        em.stream(f"tl{t_}2", "sync")
        em.stream(f"ts{t_}", "sync")

    kX = [f"X{c}" for c in range(NCH)]
    kXN = [[f"XN{f}c{c}" for c in range(NCH)] for f in range(KC)]
    kSG = [[f"SG{m}g{g}" for g in range(4)] for m in range(4)]
    kH = [[f"H{hc}g{g}" for g in range(4)] for hc in range(8)]
    kACC = [[f"ACC{c}n{n}" for n in range(2)] for c in range(NCH)]
    kPSB = [f"PSB{i}" for i in range(8)]
    ALL_SG = [k for row in kSG for k in row]
    ALL_H = [k for row in kH for k in row]
    ALL_ACC = [k for row in kACC for k in row]
    kVSC = ["VSC0", "VSC1"]
    kSTSB = ["STSB0", "STSB1"]
    kQC = ["QC0", "QC1"]
    kYC = ["YC0", "YC1"]
    kSGC = ["SGC0", "SGC1", "SGC2"]
    kTYC = ["TYC0", "TYC1", "TYC2"]
    kY2C = ["Y2C0", "Y2C1"]
    kY2FM = ["Y2FM0", "Y2FM1"]
    B_H = kVSC + kSTSB + kQC + kYC + kSGC
    B_ACC = ["WOT"] + kTYC + kY2C + kY2FM
    PREPC_KEYS = ["PREPC", "PREPC2", "PREPC3", "PREPC4", "PREPC5", "PREPC6"]
    PREP_SG_KEYS = ([f"tst{x}{a}" for x in ("g", "u") for a in range(2)]
                    + [f"tstd{a}_{i}" for a in range(2) for i in range(3)]
                    + [f"tstd{a}" for a in range(2)])
    WAR_ACC_B = ALL_ACC
    WAR_ACC_A = B_ACC + PREPC_KEYS

    em.dma("cl", lambda g: g.dma_start(out=GRIDC, in_=grid_d), writes=["PREPC"])
    em.dma("cl", lambda g: g.dma_start(out=MASKC, in_=mask_d), writes=["PREPC2"])
    em.dma("cl", lambda g: g.dma_start(out=ONESC, in_=ones_d), writes=["PREPC3"])
    em.dma("cl", lambda g: g.dma_start(out=SHEADC, in_=shead_d), writes=["PREPC4"])
    em.dma("cl", lambda g: g.dma_start(out=GAMR, in_=gam_d), writes=["PREPC5"])
    em.dma("cl", lambda g: g.dma_start(out=BETR, in_=bet_d), writes=["PREPC6"])
    em.dma("cl", lambda g: g.dma_start(out=SM3[:], in_=ab_r_d), writes=["SM3"])
    em.dma("cl", lambda g: g.dma_start(out=SM4[:], in_=ab_c_d), writes=["SM4"])
    em.dma("cl", lambda g: g.dma_start(out=IDN[:], in_=ident_d), writes=["IDN"])

    def prep_cast(tag):
        em.dma("pc", lambda g, a=nat[tag], b=w_dram[tag]: g.dma_start(out=a, in_=b),
               writes=[f"nat_{tag}"])

    def prep_T_gu(tag, scr):
        sx = "g" if tag == "wgate" else "u"
        for kc in range(KC):
            slot = SG[:, (kc % 2) * HID:(kc % 2) * HID + HID]
            skey = f"tst{sx}{kc % 2}"
            em.dma(f"tl{sx}" + ("2" if kc % 2 else ""),
                   lambda s, a=slot, b=nat[tag][:, 128 * kc:128 * (kc + 1)]:
                   s.dma_start(out=a, in_=b, transpose=True),
                   reads=[f"nat_{tag}"], writes=[skey],
                   war=ALL_SG + ["WGT"] + PREP_SG_KEYS)
            dst = scr[:, :, kc, :]
            srcv = slot.rearrange("p (j h) -> p j h", h=SLAB)
            em.dma(f"ts{sx}", lambda s, a=dst, b=srcv: s.dma_start(out=a, in_=b),
                   reads=[skey], writes=[f"scrT_{tag}_{kc}"])
    def prep_T_d():
        for h0 in range(0, HID // 128, 3):
            half = (h0 // 3) % 2
            slot = SG[:, half * HID:half * HID + 3 * D]
            skey = f"tstd{half}"
            for i in range(3):
                hc = h0 + i
                em.dma("tld" + ("2" if hc % 2 else ""),
                       lambda s, a=slot[:, D * i:D * (i + 1)],
                       b=nat["wdown"][:, 128 * hc:128 * (hc + 1)]:
                       s.dma_start(out=a, in_=b, transpose=True),
                       reads=["nat_wdown"], writes=[f"{skey}_{i}"],
                       war=ALL_SG + ["WGT", skey] + PREP_SG_KEYS)
            for i in range(3):
                hc = h0 + i
                dst = scr_d[:, hc // 8, hc % 8, :]
                em.dma("tsd", lambda s, a=dst, b=slot[:, D * i:D * (i + 1)]:
                       s.dma_start(out=a, in_=b),
                       reads=[f"{skey}_{i}"], writes=[f"scrT_wdown_{hc}"])

    def prep_T_attn():
        for tag in ("wq", "wv", "wgt", "wo"):
            for kc in range(KC):
                slot = TSTB[:, (kc % 2) * D:(kc % 2) * D + D]
                skey = f"tstb{kc % 2}"
                em.dma("tla" + ("2" if kc % 2 else ""),
                       lambda s, a=slot, b=nat[tag][:, 128 * kc:128 * (kc + 1)]:
                       s.dma_start(out=a, in_=b, transpose=True),
                       reads=[f"nat_{tag}"], writes=[skey])
                dst = scr_a[tag][:, kc, :]
                em.dma("tsa", lambda s, a=dst, b=slot: s.dma_start(out=a, in_=b),
                       reads=[skey], writes=[f"scrT_{tag}_{kc}"])

    def build_consts():
        em.op("scalar", lambda s: s.activation(SM2[:], SM4[:], AF.Sigmoid),
              reads=["SM4"], writes=["SM2"])
        em.op("scalar", lambda s: s.activation(SM1[:], SM3[:], AF.Sigmoid),
              reads=["SM3"], writes=["SM1"])
        em.op("scalar", lambda s: s.activation(SM1[:], SM1[:], AF.Ln),
              reads=["SM1"], writes=["SM1"])
        em.group("tensor", [(lambda t: t.matmul(PSB[0][:, 0:NH], ONESC, SM1[:],
                                                start=True, stop=True),
                             ["SM1", "PREPC3"])], writes=["PSB0"])
        em.op("vector", lambda v: v.tensor_copy(LLREP[:], PSB[0][:, 0:NH]),
              reads=["PSB0"], writes=["LLREP"])
        for n in range(2):
            em.group("tensor", [(lambda t, n=n: t.matmul(
                PSB[1][0:1, :], SM2[:, 0:1], SHEADC[:, 512 * n:512 * (n + 1)],
                start=True, stop=True), ["SM2", "PREPC4"])], writes=["PSB1"])
            em.op("vector", lambda v, n=n: v.tensor_copy(
                LVROW[0:1, 512 * n:512 * (n + 1)], PSB[1][0:1, :]),
                reads=["PSB1"], writes=[f"LVROW{n}"])
        for n in range(2):
            em.group("tensor", [(lambda t, n=n: t.matmul(
                PSB[2][:, :], ONESC, LVROW[0:1, 512 * n:512 * (n + 1)],
                start=True, stop=True), [f"LVROW{n}", "PREPC3"])], writes=["PSB2"])
            em.op("vector", lambda v, n=n: v.tensor_copy(
                LAMV[:, 512 * n:512 * (n + 1)], PSB[2][:, :]),
                reads=["PSB2"], writes=["LAMV"])
        for src, dst, key, rk in ((GAMR, GREP, "GREP", "PREPC5"),
                                  (BETR, BREP, "BREP", "PREPC6")):
            for n in range(2):
                em.group("tensor", [(lambda t, src=src, n=n: t.matmul(
                    PSB[3][:, :], ONESC, src[:, 512 * n:512 * (n + 1)],
                    start=True, stop=True), [rk, "PREPC3"])], writes=["PSB3"])
                em.op("vector", lambda v, dst=dst, n=n: v.tensor_copy(
                    dst[:, 512 * n:512 * (n + 1)], PSB[3][:, :]),
                    reads=["PSB3"], writes=[key])
        for h in range(NH):
            em.op("scalar", lambda s, h=h: s.activation(
                LBUF[:, 128 * h:128 * (h + 1)], GRIDC, AF.Exp,
                scale=LLREP[:, h:h + 1]),
                reads=["LLREP", "PREPC"], writes=[f"LB{h}"])
            em.op("vector", lambda v, h=h: v.tensor_tensor(
                LBUF[:, 128 * h:128 * (h + 1)], LBUF[:, 128 * h:128 * (h + 1)],
                MASKC, ALU.mult),
                reads=[f"LB{h}", "PREPC2"], writes=[f"LB{h}"])
        for h in range(NH):
            em.op("scalar", lambda s, h=h: s.activation(
                LAMR[:, h:h + 1], GRIDC[:, 127:128], AF.Exp,
                scale=LLREP[:, h:h + 1], bias=LLREP[:, h:h + 1]),
                reads=["LLREP", "PREPC"], writes=["LAMR"])

    st_cols = dict(ssum=0, ssq=16, mu=32, musq=48, var=64, sd=80, rstd=96)

    def stat(namecol, c=None):
        base = st_cols[namecol]
        if c is None:
            return STATS[:, base:base + 16]
        return STATS[:, base + c:base + c + 1]

    MAGIC = 0x5f3759df

    def emit_rsqrt(varcol, varkey, wcol, wkey, tcol, tkey, zcol, zkey):
        em.op("vector", lambda v: v.tensor_scalar(wcol, varcol, float(EPS), None, ALU.add),
              reads=[varkey], writes=[wkey])
        wi, ti, zi = wcol.bitcast(I32), tcol.bitcast(I32), zcol.bitcast(I32)
        em.op("vector", lambda v: v.tensor_scalar(ti, wi, 1, None, ALU.arith_shift_right),
              reads=[wkey], writes=[tkey])
        em.op("vector", lambda v: v.tensor_scalar(zi, ti, -1, MAGIC, ALU.mult, ALU.add),
              reads=[tkey], writes=[zkey])
        for _ in range(3):
            em.op("vector", lambda v: v.tensor_tensor(tcol, zcol, zcol, ALU.mult),
                  reads=[zkey], writes=[tkey])
            em.op("vector", lambda v: v.tensor_tensor(tcol, tcol, wcol, ALU.mult),
                  reads=[tkey, wkey], writes=[tkey])
            em.op("vector", lambda v: v.tensor_scalar(tcol, tcol, -0.5, 1.5,
                                                      ALU.mult, ALU.add),
                  reads=[tkey], writes=[tkey])
            em.op("vector", lambda v: v.tensor_tensor(zcol, zcol, tcol, ALU.mult),
                  reads=[zkey, tkey], writes=[zkey])

    def ln_stats(c):
        xc = X[:, D * c:D * (c + 1)]
        em.op("vector", lambda v, xc=xc, c=c: v.reduce_sum(
            stat("ssum", c), xc, axis=AX.X), reads=[kX[c]], writes=["ST_ssum"])
        sq = TMPTM[:, D * (c % 2):D * (c % 2 + 1)]
        em.op("scalar", lambda s, xc=xc, c=c, sq=sq:
              s.activation(sq, xc, AF.Square, accum_out=stat("ssq", c)),
              reads=[kX[c]], writes=["ST_ssq", f"TT{c % 2}"])

    def layer_norm_to_xn(prestats=True):
        if not prestats:
            for c in range(NCH):
                ln_stats(c)
        em.op("vector", lambda v: v.tensor_scalar(
            stat("mu"), stat("ssum"), 1.0 / D, None, ALU.mult),
            reads=["ST_ssum"], writes=["ST_mu"])
        em.op("vector", lambda v: v.tensor_tensor(
            stat("musq"), stat("mu"), stat("mu"), ALU.mult),
            reads=["ST_mu"], writes=["ST_musq"])
        em.op("vector", lambda v: v.scalar_tensor_tensor(
            stat("var"), stat("ssq"), 1.0 / D, stat("musq"), ALU.mult, ALU.subtract),
            reads=["ST_ssq", "ST_musq"], writes=["ST_var"])
        emit_rsqrt(stat("var"), "ST_var", stat("sd"), "ST_sd",
                   stat("musq"), "ST_musq", stat("rstd"), "ST_rstd")
        for c in range(NCH):
            xc = X[:, D * c:D * (c + 1)]
            tm = TMPTM[:, D * (c % 2):D * (c % 2 + 1)]
            em.op("vector", lambda v, xc=xc, tm=tm, c=c: v.tensor_scalar(
                tm, xc, stat("mu", c), stat("rstd", c), ALU.subtract, ALU.mult),
                reads=[kX[c], "ST_mu", "ST_rstd"], writes=[f"TT{c % 2}"])
            bank, bkey = PSB7, "PSB7"
            items = [(lambda t, tm=tm, f=f, bank=bank: t.transpose(
                bank[:, 128 * f:128 * (f + 1)], tm[:, 128 * f:128 * (f + 1)], IDN[:]),
                [f"TT{c % 2}", "IDN"]) for f in range(KC)]
            em.group("tensor", items, writes=[bkey])
            xnv = XN[:].rearrange("p (f t) -> p f t", t=S)[:, :, 128 * c:128 * (c + 1)]
            bv = bank[:].rearrange("p (f t) -> p f t", t=128)
            em.op("scalar", lambda s, xnv=xnv, bv=bv: s.activation(xnv, bv, AF.Copy),
                  reads=[bkey], writes=[kXN[f][c] for f in range(KC)])

    def ffn_phase(tap=False, prestats=True, next_stats=True, tail_cb=None,
                  skip_gw0=False):
        layer_norm_to_xn(prestats=prestats)
        if tap:
            em.dma("st", lambda g: g.dma_start(out=dbg_xn, in_=XN[:]),
                   reads=[k for row in kXN for k in row], writes=["DBGXN"])
        nsl = NSLAB if DBG_SLABS is None else DBG_SLABS
        for j in range(nsl):
            li = 3 * j
            slots = [(WA, "WSLOTA"), (WB, "WSLOTB")]
            wg_sl, wg_k = slots[li % 2]
            wu_sl, wu_k = slots[(li + 1) % 2]
            wd_sl, wd_k = slots[(li + 2) % 2]
            if not (skip_gw0 and j == 0):
                em.dma("wl", lambda s, a=wg_sl, j=j: s.dma_start(
                    out=a[:].rearrange("p (k h) -> p k h", h=SLAB), in_=scr_g[:, j]),
                    reads=[f"scrT_wgate_{kc}" for kc in range(KC)], writes=[wg_k])
                em.dma("wl", lambda s, a=wu_sl, j=j: s.dma_start(
                    out=a[:].rearrange("p (k h) -> p k h", h=SLAB), in_=scr_u[:, j]),
                    reads=[f"scrT_wup_{kc}" for kc in range(KC)], writes=[wu_k])
            if tap and j == 0:
                em.dma("st", lambda g, a=wg_sl: g.dma_start(out=dbg_wa, in_=a[:]),
                       reads=[wg_k], writes=["DBGWA"])
            for hs in range(2):
                for g in range(4):
                    for m4 in range(4):
                        m = 4 * hs + m4
                        bank = (g * 4 + m4) % 2
                        items = [(lambda t, k=k, m=m, g=g, wg_sl=wg_sl, bank=bank:
                                  t.matmul(PSB[bank][:],
                                           wg_sl[:, SLAB * k + 128 * m:SLAB * k + 128 * (m + 1)],
                                           XN[:, S * k + 512 * g:S * k + 512 * (g + 1)],
                                           start=(k == 0), stop=(k == KC - 1)),
                                  [wg_k] + [kXN[k][4 * g + i] for i in range(4)])
                                 for k in range(KC)]
                        em.group("tensor", items, writes=[kPSB[bank]])
                        if tap and j == 0 and hs == 0 and g == 0 and m4 == 0:
                            dps = HF[:, 4096:4608]
                            em.op("vector", lambda v, bank=bank: v.tensor_copy(
                                dps, PSB[bank][:]),
                                reads=[kPSB[bank]],
                                writes=[kH[4][i] for i in range(4)], war=B_H)
                            em.dma("st", lambda g2: g2.dma_start(out=dbg_ps, in_=dps),
                                   reads=[kH[4][0]], writes=["DBGPS"])
                        dst = SG[:, S * m4 + 512 * g:S * m4 + 512 * (g + 1)]
                        em.op("scalar", lambda s, dst=dst, bank=bank:
                              s.activation(dst, PSB[bank][:], AF.Silu,
                                           scale=np32(1.0 / 127.0)),
                              reads=[kPSB[bank]], writes=[kSG[m4][g]],
                              war=["WGT"] + PREP_SG_KEYS)
                for g in range(4):
                    for m4 in range(4):
                        m = 4 * hs + m4
                        hc = m
                        bank = 2 + (g * 4 + m4) % 2
                        items = [(lambda t, k=k, m=m, g=g, wu_sl=wu_sl, bank=bank:
                                  t.matmul(PSB[bank][:],
                                           wu_sl[:, SLAB * k + 128 * m:SLAB * k + 128 * (m + 1)],
                                           XN[:, S * k + 512 * g:S * k + 512 * (g + 1)],
                                           start=(k == 0), stop=(k == KC - 1)),
                                  [wu_k] + [kXN[k][4 * g + i] for i in range(4)])
                                 for k in range(KC)]
                        em.group("tensor", items, writes=[kPSB[bank]])
                        dst = H[:, S * hc + 512 * g:S * hc + 512 * (g + 1)]
                        src0 = SG[:, S * m4 + 512 * g:S * m4 + 512 * (g + 1)]
                        em.op("vector", lambda v, dst=dst, src0=src0, bank=bank:
                              v.tensor_tensor(dst, src0, PSB[bank][:], ALU.mult),
                              reads=[kSG[m4][g], kPSB[bank]], writes=[kH[hc][g]],
                              war=B_H)
            if tap and j == 0:
                em.dma("st", lambda g: g.dma_start(out=dbg_sg, in_=SG[:]),
                       reads=ALL_SG, writes=["DBGSG"])
                em.dma("st", lambda g: g.dma_start(out=dbg_h, in_=H[:]),
                       reads=ALL_H, writes=["DBGH"])
            em.dma("wl", lambda s, a=wd_sl, j=j: s.dma_start(
                out=a[:].rearrange("p (k h) -> p k h", h=D), in_=scr_d[:, j]),
                reads=[f"scrT_wdown_{8 * j + i}" for i in range(8)], writes=[wd_k])
            for c in range(NCH):
                for n in range(2):
                    bank = 4 + (c * 2 + n) % 3
                    items = [(lambda t, hc=hc, c=c, n=n, wd_sl=wd_sl, bank=bank:
                              t.matmul(PSB[bank][:],
                                       H[:, S * hc + 128 * c:S * hc + 128 * (c + 1)],
                                       wd_sl[:, D * hc + 512 * n:D * hc + 512 * (n + 1)],
                                       start=(hc == 0), stop=(hc == 7)),
                              [wd_k, kH[hc][c // 4]])
                             for hc in range(8)]
                    em.group("tensor", items, writes=[kPSB[bank]])
                    dst = ACC[:, D * c + 512 * n:D * c + 512 * (n + 1)]
                    if j == 0:
                        em.op("scalar", lambda s, dst=dst, bank=bank:
                              s.activation(dst, PSB[bank][:], AF.Copy, scale=SFIN),
                              reads=[kPSB[bank]], writes=[kACC[c][n]], war=WAR_ACC_A)
                    else:
                        em.op("vector", lambda v, dst=dst, bank=bank:
                              v.scalar_tensor_tensor(dst, PSB[bank][:], SFIN, dst,
                                                     ALU.mult, ALU.add),
                              reads=[kPSB[bank], kACC[c][n]], writes=[kACC[c][n]])
                if j == NSLAB - 1 or (DBG_SLABS is not None and j == DBG_SLABS - 1):
                    xc = X[:, D * c:D * (c + 1)]
                    ac = ACC[:, D * c:D * (c + 1)]
                    em.op("vector", lambda v, xc=xc, ac=ac: v.tensor_tensor(
                        xc, xc, ac, ALU.add),
                        reads=[kX[c], kACC[c][0], kACC[c][1]], writes=[kX[c]])
                    if next_stats:
                        ln_stats(c)
                    if tail_cb is not None:
                        tail_cb(c)
        if tap:
            em.dma("st", lambda g: g.dma_start(out=dbg_acc, in_=ACC[:]),
                   reads=ALL_ACC, writes=["DBGACC"])

    def attn_phase():
        layer_norm_to_xn(prestats=True)
        em.dma("wl", lambda s: s.dma_start(
            out=WA[:].rearrange("p (k h) -> p k h", h=D), in_=scr_a["wq"]),
            reads=[f"scrT_wq_{kc}" for kc in range(KC)], writes=["WSLOTA"])
        em.dma("wl", lambda s: s.dma_start(
            out=WB[:].rearrange("p (k h) -> p k h", h=D), in_=scr_a["wv"]),
            reads=[f"scrT_wv_{kc}" for kc in range(KC)], writes=["WSLOTB"])
        em.dma("wl", lambda s: s.dma_start(
            out=SG[:].rearrange("p (k h) -> p k h", h=D), in_=scr_a["wgt"]),
            reads=[f"scrT_wgt_{kc}" for kc in range(KC)], writes=["WGT"],
            war=ALL_SG + PREP_SG_KEYS)
        em.dma("wl", lambda s: s.dma_start(
            out=WOT.rearrange("p (k h) -> p k h", h=D), in_=scr_a["wo"]),
            reads=[f"scrT_wo_{kc}" for kc in range(KC)], writes=["WOT"],
            war=[kACC[c][n] for c in range(8) for n in range(2)]
                + PREPC_KEYS)

        pslot = [0]

        def proj(c, wslot, wkey):
            banks = []
            for n in range(2):
                bank = 4 + pslot[0] % 3
                pslot[0] += 1
                banks.append(bank)
                items = [(lambda t, k=k, n=n, c=c, wslot=wslot, bank=bank:
                          t.matmul(PSB[bank][:],
                                   XN[:, S * k + 128 * c:S * k + 128 * (c + 1)],
                                   wslot[:, D * k + 512 * n:D * k + 512 * (n + 1)],
                                   start=(k == 0), stop=(k == KC - 1)),
                          [wkey, kXN[k][c]]) for k in range(KC)]
                em.group("tensor", items, writes=[kPSB[bank]])
            return banks

        def part1a(c):
            sl = c % 2
            s3 = c % 3
            vb = proj(c, WB, "WSLOTB")
            for n in range(2):
                em.op("scalar", lambda s, sl=sl, n=n, b=vb[n]: s.activation(
                    VSC[sl][:, 512 * n:512 * (n + 1)], PSB[b][:], AF.Copy),
                    reads=[kPSB[vb[n]]], writes=[kVSC[sl]], war=ALL_H)
            if c > 0:
                em.op("gpsimd", lambda g, sl=sl: g.tensor_tensor(
                    VSC[sl][0:1, :], VSC[sl][0:1, :], CROW[0:1, :], ALU.add),
                    reads=[kVSC[sl], "CROW"], writes=[kVSC[sl]])
            qb = proj(c, WA, "WSLOTA")
            for n in range(2):
                em.op("scalar", lambda s, sl=sl, n=n, b=qb[n]: s.activation(
                    QC[sl][:, 512 * n:512 * (n + 1)], PSB[b][:], AF.Copy),
                    reads=[kPSB[qb[n]]], writes=[kQC[sl]], war=ALL_H)
            gb = proj(c, SG, "WGT")
            for n in range(2):
                em.op("scalar", lambda s, s3=s3, n=n, b=gb[n]: s.activation(
                    SGC[s3][:, 512 * n:512 * (n + 1)], PSB[b][:], AF.Silu),
                    reads=[kPSB[gb[n]]], writes=[kSGC[s3]], war=ALL_H)
            sb0 = 2 * sl
            items = []
            for h in range(NH):
                bank = sb0 + (1 if h >= 8 else 0)
                off = 64 * h - 512 * (h // 8)
                items.append((lambda t, h=h, sl=sl, bank=bank, off=off:
                              t.matmul(PSB[bank][:, off:off + 64],
                                       LBUF[:, 128 * h:128 * (h + 1)],
                                       VSC[sl][:, 64 * h:64 * (h + 1)],
                                       start=True, stop=True),
                              [kVSC[sl], f"LB{h}"]))
            em.group("tensor", items, writes=[kPSB[sb0], kPSB[sb0 + 1]])

        def part1b(c):
            sl = c % 2
            s3 = c % 3
            sb0 = 2 * sl
            for n in range(2):
                em.op("vector", lambda v, sl=sl, n=n, sb0=sb0: v.tensor_tensor(
                    YC[sl][:, 512 * n:512 * (n + 1)], PSB[sb0 + n][:],
                    QC[sl][:, 512 * n:512 * (n + 1)], ALU.mult),
                    reads=[kPSB[sb0 + n], kQC[sl]], writes=[kYC[sl]],
                    war=ALL_H)
            yb = YC[sl]
            ysum = STATS[:, 112 + sl:113 + sl]
            ysq = STATS[:, 114 + sl:115 + sl]
            ymu = STATS[:, 116 + sl:117 + sl]
            yvar = STATS[:, 118 + sl:119 + sl]
            ysd = STATS[:, 120 + sl:121 + sl]
            yrs = STATS[:, 122 + sl:123 + sl]
            ymsq = STATS[:, 124 + sl:125 + sl]
            em.op("vector", lambda v, yb=yb, ysum=ysum: v.reduce_sum(
                ysum, yb, axis=AX.X), reads=[kYC[sl]], writes=[f"YSS{sl}"])
            sq = TMPTM[:, D * sl:D * (sl + 1)]
            em.op("scalar", lambda s, yb=yb, sq=sq, ysq=ysq:
                  s.activation(sq, yb, AF.Square, accum_out=ysq),
                  reads=[kYC[sl]], writes=[f"YSQ{sl}", f"TT{sl}"])
            em.op("vector", lambda v, ysum=ysum, ymu=ymu: v.tensor_scalar(
                ymu, ysum, 1.0 / D, None, ALU.mult),
                reads=[f"YSS{sl}"], writes=[f"YMU{sl}"])
            em.op("vector", lambda v, ymu=ymu, ymsq=ymsq: v.tensor_tensor(
                ymsq, ymu, ymu, ALU.mult), reads=[f"YMU{sl}"], writes=[f"YMSQ{sl}"])
            em.op("vector", lambda v, ysq=ysq, ymsq=ymsq, yvar=yvar:
                  v.scalar_tensor_tensor(yvar, ysq, 1.0 / D, ymsq,
                                         ALU.mult, ALU.subtract),
                  reads=[f"YSQ{sl}", f"YMSQ{sl}"], writes=[f"YVAR{sl}"])
            emit_rsqrt(yvar, f"YVAR{sl}", ysd, f"YSD{sl}",
                       ymsq, f"YMSQ{sl}", yrs, f"YRS{sl}")
            ty = TYC[s3]
            em.op("vector", lambda v, yb=yb, ty=ty, ymu=ymu, yrs=yrs:
                  v.tensor_scalar(ty, yb, ymu, yrs, ALU.subtract, ALU.mult),
                  reads=[kYC[sl], f"YMU{sl}", f"YRS{sl}"], writes=[kTYC[s3]],
                  war=WAR_ACC_B)
            em.op("vector", lambda v, ty=ty: v.tensor_tensor(ty, ty, GREP[:], ALU.mult),
                  reads=[kTYC[s3], "GREP"], writes=[kTYC[s3]])
            em.op("vector", lambda v, ty=ty: v.tensor_tensor(ty, ty, BREP[:], ALU.add),
                  reads=[kTYC[s3], "BREP"], writes=[kTYC[s3]])

        def carrymm(c):
            sl = c % 2
            pb0 = 2 * ((c + 1) % 2)
            items = []
            for h in range(NH):
                bank = pb0 + (1 if h >= 8 else 0)
                off = 64 * h - 512 * (h // 8)
                items.append((lambda t, h=h, sl=sl, bank=bank, off=off:
                              t.matmul(PSB[bank][0:1, off:off + 64],
                                       LAMR[:, h:h + 1],
                                       VSC[sl][:, 64 * h:64 * (h + 1)],
                                       start=True, stop=True),
                              [kVSC[sl], "LAMR"]))
            em.group("tensor", items, writes=[kPSB[pb0], kPSB[pb0 + 1]])
            for n in range(2):
                em.op("scalar", lambda s, n=n, pb0=pb0: s.activation(
                    CROW[0:1, 512 * n:512 * (n + 1)], PSB[pb0 + n][0:1, :], AF.Copy),
                    reads=[kPSB[pb0 + n]], writes=["CROW"],
                    war=["LVROW0", "LVROW1"])

        def part2(c):
            sl = c % 2
            s3 = c % 3
            ty = TYC[s3]
            y2 = Y2C[sl]
            em.op("vector", lambda v, y2=y2, ty=ty, s3=s3: v.tensor_tensor(
                y2, ty, SGC[s3], ALU.mult),
                reads=[kTYC[s3], kSGC[s3]], writes=[kY2C[sl]], war=WAR_ACC_B)
            bank, bkey = PSB7, "PSB7"
            items = [(lambda t, y2=y2, f=f, bank=bank: t.transpose(
                bank[:, 128 * f:128 * (f + 1)], y2[:, 128 * f:128 * (f + 1)], IDN[:]),
                [kY2C[sl], "IDN"]) for f in range(KC)]
            em.group("tensor", items, writes=[bkey])
            em.op("scalar", lambda s, sl=sl, bank=bank: s.activation(
                Y2FM[sl], bank[:], AF.Copy),
                reads=[bkey], writes=[kY2FM[sl]], war=WAR_ACC_B)
            for n in range(2):
                bank2 = 4 + pslot[0] % 3
                pslot[0] += 1
                items = [(lambda t, k=k, n=n, sl=sl, bank2=bank2:
                          t.matmul(PSB[bank2][:],
                                   Y2FM[sl][:, 128 * k:128 * (k + 1)],
                                   WOT[:, D * k + 512 * n:D * k + 512 * (n + 1)],
                                   start=(k == 0), stop=(k == KC - 1)),
                          ["WOT", kY2FM[sl]]) for k in range(KC)]
                em.group("tensor", items, writes=[kPSB[bank2]])
                xc = X[:, D * c + 512 * n:D * c + 512 * (n + 1)]
                em.op("vector", lambda v, xc=xc, bank2=bank2: v.tensor_tensor(
                    xc, xc, PSB[bank2][:], ALU.add),
                    reads=[kX[c], kPSB[bank2]], writes=[kX[c]])
            ln_stats(c)

        for c in range(NCH + 2):
            if c < NCH:
                part1a(c)
            if c >= 1:
                part1b(c - 1)
            if c < NCH - 1:
                carrymm(c)
            if c >= 2:
                part2(c - 2)

    def load_q(si, q):
        src = x_d[si][512 * q:512 * (q + 1), :].rearrange("(c p) d -> p c d", p=128)
        dstv = X[:, 4 * D * q:4 * D * (q + 1)].rearrange("p (c d) -> p c d", d=D)
        em.dma(f"io{q}", lambda g, a=dstv, b=src: g.dma_start(out=a, in_=b),
               writes=[kX[4 * q + i] for i in range(4)])

    load_q(0, 0)
    prep_cast("wgate")
    prep_cast("wup")
    prep_cast("wdown")
    for q in range(1, 4):
        load_q(0, q)
    for w in ("wq", "wv", "wgt", "wo"):
        prep_cast(w)
    build_consts()
    prep_T_gu("wgate", scr_g)
    em.dma("wl", lambda s: s.dma_start(
        out=WA[:].rearrange("p (k h) -> p k h", h=SLAB), in_=scr_g[:, 0]),
        reads=[f"scrT_wgate_{kc}" for kc in range(KC)], writes=["WSLOTA"])
    prep_T_gu("wup", scr_u)
    em.dma("wl", lambda s: s.dma_start(
        out=WB[:].rearrange("p (k h) -> p k h", h=SLAB), in_=scr_u[:, 0]),
        reads=[f"scrT_wup_{kc}" for kc in range(KC)], writes=["WSLOTB"])
    prep_T_d()
    for s in range(SEQS):
        if "A" in PHASES:
            ffn_phase(tap=(DBG_TAPS and s == 0), prestats=(s > 0),
                      next_stats=("B" in PHASES or "C" in PHASES),
                      skip_gw0=(s == 0))
        if s == 0:
            prep_T_attn()
        if "B" in PHASES:
            attn_phase()
        def boundary_cb(c, s=s):
            if c % 4 != 3:
                return
            q = c // 4
            dso = o_d[s][512 * q:512 * (q + 1), :].rearrange("(c p) d -> p c d", p=128)
            srcv = X[:, 4 * D * q:4 * D * (q + 1)].rearrange("p (c d) -> p c d", d=D)
            em.dma(f"io{q}", lambda g, a=dso, b=srcv: g.dma_start(out=a, in_=b),
                   reads=[kX[4 * q + i] for i in range(4)], writes=[f"OUTD{q}"])
            if s + 1 < SEQS:
                src = x_d[s + 1][512 * q:512 * (q + 1), :].rearrange("(c p) d -> p c d", p=128)
                dstv = X[:, 4 * D * q:4 * D * (q + 1)].rearrange("p (c d) -> p c d", d=D)
                em.dma(f"io{q}", lambda g, a=dstv, b=src: g.dma_start(out=a, in_=b),
                       writes=[kX[4 * q + i] for i in range(4)])
                if q >= 1:
                    for i in range(4):
                        ln_stats(4 * (q - 1) + i)
        if "C" in PHASES:
            ffn_phase(prestats=("B" in PHASES), next_stats=False,
                      tail_cb=boundary_cb)
            if s + 1 < SEQS:
                for i in range(4):
                    ln_stats(12 + i)
        else:
            for c in range(NCH):
                boundary_cb(c)
            if s + 1 < SEQS:
                for i in range(4):
                    ln_stats(12 + i)

    em.op("gpsimd", lambda g: g.nop(),
          deps_extra=[("d", f"io{q}", 2 * SEQS) for q in range(4)])

    em.finalize_and_emit(nc, es)
    es.close()
    return nc


def _consts():
    tt = np.arange(128)
    g = (tt[None, :] - tt[:, None]).astype(np.float32)
    msk = (g >= 0).astype(np.float32)
    g = np.where(g >= 0, g, -1.0).astype(np.float32)
    shead = np.zeros((NH, D), np.float32)
    for h in range(NH):
        shead[h, HD * h:HD * (h + 1)] = 1.0
    return dict(gridc=g, maskc=msk, onesc=np.ones((1, 128), np.float32),
                sheadc=shead, identc=np.eye(128, dtype=ml_dtypes.bfloat16))


def kernel(L0_attn_x, L0_attn_Wq, L0_attn_Wv, L0_attn_Wo, L0_attn_Wg,
           L0_attn_beta, L0_attn_ln_gamma, L0_attn_ln_beta,
           L0_ffn_w_gate, L0_ffn_w_up, L0_ffn_w_down):
    x = np.ascontiguousarray(np.asarray(L0_attn_x, dtype=np.float32))
    shared = dict(
        wq=np.ascontiguousarray(L0_attn_Wq, dtype=np.float32),
        wv=np.ascontiguousarray(L0_attn_Wv, dtype=np.float32),
        wo=np.ascontiguousarray(L0_attn_Wo, dtype=np.float32),
        wgt=np.ascontiguousarray(L0_attn_Wg, dtype=np.float32),
        wgate=np.ascontiguousarray(L0_ffn_w_gate, dtype=np.int32),
        wup=np.ascontiguousarray(L0_ffn_w_up, dtype=np.int32),
        wdown=np.ascontiguousarray(L0_ffn_w_down, dtype=np.int32),
        gam=np.ascontiguousarray(np.asarray(L0_attn_ln_gamma, np.float32).reshape(1, D)),
        bet=np.ascontiguousarray(np.asarray(L0_attn_ln_beta, np.float32).reshape(1, D)),
        ab_r=np.ascontiguousarray(np.asarray(L0_attn_beta, np.float32).reshape(1, NH)),
        ab_c=np.ascontiguousarray(np.asarray(L0_attn_beta, np.float32).reshape(NH, 1)),
        **_consts(),
    )
    nc = build_program()
    in_maps = []
    for c in range(NCORES):
        m = dict(shared)
        m["x"] = np.ascontiguousarray(x[SEQS * c:SEQS * (c + 1)])
        in_maps.append(m)
    res = run_bass_kernel_spmd(nc, in_maps, core_ids=list(range(NCORES)))
    if DBG_TAPS:
        global DBG_RESULT
        DBG_RESULT = res.results[0]
    return np.concatenate([r["o"] for r in res.results], axis=0)


# version 17
# speedup vs baseline: 1.0103x; 1.0103x over previous
import numpy as np
import ml_dtypes
from contextlib import ExitStack
import concourse.bass as bass
import concourse.mybir as mybir
from concourse.bass_utils import run_bass_kernel_spmd

F32 = mybir.dt.float32
BF16 = mybir.dt.bfloat16
I32 = mybir.dt.int32
AF = mybir.ActivationFunctionType
ALU = mybir.AluOpType
AX = mybir.AxisListType

NCORES = 8
SEQS = 4
S = 2048
D = 1024
HID = 3072
NH = 16
HD = 64
NCH = S // 128
KC = D // 128
SLAB = 1024
NSLAB = HID // SLAB
EPS = 1e-5
SFIN = 0.5 / (127.0 * 127.0)
np32 = np.float32

PHASES = "ABC"
DBG_SLABS = None
DBG_TAPS = False


class Em:
    ENGS = ("tensor", "vector", "scalar", "gpsimd", "sync")

    def __init__(self):
        self.ins = {e: [] for e in self.ENGS}
        self.res = {}
        self.streams = {}
        self.dcnt = {}

    def stream(self, name, eng):
        self.streams[name] = eng
        self.dcnt[name] = 0

    def _rdeps(self, key, eng):
        r = self.res.get(key)
        deps = set()
        if r:
            if r["w"] is not None:
                deps.add(r["w"])
            if key.startswith("PSB"):
                for rv in r["r"]:
                    if rv[0] == "d" or rv[1] != eng:
                        deps.add(rv)
        return deps

    def _wdeps(self, key):
        r = self.res.get(key)
        deps = set()
        if r:
            if r["w"] is not None:
                deps.add(r["w"])
            deps.update(r["r"])
        return deps

    def op(self, eng, fn, reads=(), writes=(), war=(), deps_extra=()):
        deps = set(deps_extra)
        for k in reads:
            deps |= self._rdeps(k, eng)
        for k in writes:
            deps |= self._wdeps(k)
        for k in war:
            deps |= self._wdeps(k)
        idx = len(self.ins[eng])
        ev = ("e", eng, idx)
        self.ins[eng].append(dict(fn=fn, deps=deps, inc=False, stream=None, val=None))
        for k in reads:
            self.res.setdefault(k, dict(w=None, r=[]))["r"].append(ev)
        for k in writes:
            self.res[k] = dict(w=ev, r=[])
        return ev

    def dma(self, stream, fn, reads=(), writes=(), war=()):
        eng = self.streams[stream]
        deps = set()
        for k in reads:
            deps |= self._rdeps(k, eng)
        for k in writes:
            deps |= self._wdeps(k)
        for k in war:
            deps |= self._wdeps(k)
        if self.dcnt[stream] >= 1:
            deps.add(("d", stream, self.dcnt[stream]))
        self.dcnt[stream] += 1
        ev = ("d", stream, self.dcnt[stream])
        self.ins[eng].append(dict(fn=fn, deps=deps, inc=True, stream=stream, val=None))
        for k in reads:
            self.res.setdefault(k, dict(w=None, r=[]))["r"].append(ev)
        for k in writes:
            self.res[k] = dict(w=ev, r=[])
        return ev

    def group(self, eng, items, writes, war=()):
        deps0 = set()
        for k in writes:
            deps0 |= self._wdeps(k)
        for k in war:
            deps0 |= self._wdeps(k)
        last = None
        for i, (fn, reads) in enumerate(items):
            deps = set(deps0) if i == 0 else set()
            for k in reads:
                deps |= self._rdeps(k, eng)
            idx = len(self.ins[eng])
            ev = ("e", eng, idx)
            self.ins[eng].append(dict(fn=fn, deps=deps, inc=False, stream=None, val=None))
            for k in reads:
                self.res.setdefault(k, dict(w=None, r=[]))["r"].append(ev)
            last = ev
        for k in writes:
            self.res[k] = dict(w=last, r=[])
        return last

    def check_deadlock(self):
        ptr = {e: 0 for e in self.ENGS}
        done_val = {}
        for e in self.ENGS:
            done_val[("e", e)] = 0
        for st in self.streams:
            done_val[("d", st)] = 0
        vals = {e: [] for e in self.ENGS}
        cnt = {e: 0 for e in self.ENGS}
        dcnt2 = {st: 0 for st in self.streams}
        for e in self.ENGS:
            for inst in self.ins[e]:
                if inst["stream"] is not None:
                    dcnt2[inst["stream"]] += 1
                    vals[e].append(("d", inst["stream"], dcnt2[inst["stream"]] * 16))
                elif inst["inc"]:
                    cnt[e] += 1
                    vals[e].append(("e", e, cnt[e]))
                else:
                    cnt_e = cnt[e]
                    vals[e].append(None)
        def dep_ok(dep):
            kind, name, v = dep
            if kind == "e":
                tgt = self.ins[name][v]
                return done_val[("e", name)] >= (tgt["val"] or 0)
            return done_val[("d", name)] >= 16 * v
        for e in self.ENGS:
            c2 = 0
            for inst in self.ins[e]:
                if inst["stream"] is None and inst["inc"]:
                    c2 += 1
                    inst["val"] = c2
        progress = True
        while progress:
            progress = False
            for e in self.ENGS:
                while ptr[e] < len(self.ins[e]):
                    inst = self.ins[e][ptr[e]]
                    if all(dep_ok(d) for d in inst["deps"]):
                        mark = vals[e][ptr[e]]
                        if mark is not None:
                            if mark[0] == "d":
                                done_val[("d", mark[1])] = mark[2]
                            else:
                                done_val[("e", e)] = mark[2]
                        elif inst["inc"]:
                            pass
                        ptr[e] += 1
                        progress = True
                    else:
                        break
        stuck = {e: ptr[e] for e in self.ENGS if ptr[e] < len(self.ins[e])}
        if stuck:
            print("DEADLOCK DETECTED; engine heads:")
            for e, p in stuck.items():
                inst = self.ins[e][p]
                bad = [d for d in inst["deps"] if not dep_ok(d)]
                print(f"  {e}[{p}]: unmet deps {bad[:4]}")
            raise AssertionError("deadlock in emitted program")

    def finalize_and_emit(self, nc, es):
        for eng in self.ENGS:
            for inst in self.ins[eng]:
                best = {}
                for dep in inst["deps"]:
                    kind, name, v = dep
                    kk = (kind, name)
                    if kk not in best or v > best[kk][2]:
                        best[kk] = dep
                inst["deps"] = sorted(best.values(), key=lambda d: (d[0], str(d[1]), d[2]))
        for eng in self.ENGS:
            for inst in self.ins[eng]:
                for dep in inst["deps"]:
                    if dep[0] == "e":
                        self.ins[dep[1]][dep[2]]["inc"] = True
        for eng in self.ENGS:
            cnt = 0
            for inst in self.ins[eng]:
                if inst["stream"] is None and inst["inc"]:
                    cnt += 1
                    inst["val"] = cnt
        self.check_deadlock()
        semh = {}
        for eng in self.ENGS:
            semh[("e", eng)] = es.enter_context(nc.semaphore(f"se_{eng}"))
        for st in self.streams:
            semh[("d", st)] = es.enter_context(nc.semaphore(f"sd_{st}"))

        def resolve(dep):
            kind, name, v = dep
            if kind == "e":
                return semh[("e", name)], self.ins[name][v]["val"]
            return semh[("d", name)], 16 * v

        blk = es.enter_context(nc.Block())
        em = self

        def body(engobj, eng):
            waited = {}
            for inst in em.ins[eng]:
                for dep in inst["deps"]:
                    sem, val = resolve(dep)
                    key = id(sem)
                    if waited.get(key, 0) < val:
                        engobj.wait_ge(sem, val)
                        waited[key] = val
                bi = inst["fn"](engobj)
                if inst["stream"] is not None:
                    bi.then_inc(semh[("d", inst["stream"])], 16)
                elif inst["inc"]:
                    bi.then_inc(semh[("e", eng)], 1)

        for name, starter in (("tensor", blk.tensor), ("vector", blk.vector),
                              ("scalar", blk.scalar), ("gpsimd", blk.gpsimd),
                              ("sync", blk.sync)):
            if self.ins[name]:
                starter(lambda engobj, name=name: body(engobj, name))


def build_program():
    nc = bass.Bass("TRN2", target_bir_lowering=False)

    x_d = nc.dram_tensor("x", [SEQS, S, D], F32, kind="ExternalInput").ap()
    o_d = nc.dram_tensor("o", [SEQS, S, D], F32, kind="ExternalOutput").ap()
    if DBG_TAPS:
        dbg_xn = nc.dram_tensor("dbg_xn", [128, KC * S], F32, kind="ExternalOutput").ap()
        dbg_sg = nc.dram_tensor("dbg_sg", [128, 4 * S], F32, kind="ExternalOutput").ap()
        dbg_h = nc.dram_tensor("dbg_h", [128, 8 * S], F32, kind="ExternalOutput").ap()
        dbg_acc = nc.dram_tensor("dbg_acc", [128, NCH * D], F32, kind="ExternalOutput").ap()
        dbg_wa = nc.dram_tensor("dbg_wa", [128, KC * SLAB], F32, kind="ExternalOutput").ap()
        dbg_ps = nc.dram_tensor("dbg_ps", [128, 512], F32, kind="ExternalOutput").ap()
    w_dram = dict(
        wq=nc.dram_tensor("wq", [D, D], F32, kind="ExternalInput").ap(),
        wv=nc.dram_tensor("wv", [D, D], F32, kind="ExternalInput").ap(),
        wo=nc.dram_tensor("wo", [D, D], F32, kind="ExternalInput").ap(),
        wgt=nc.dram_tensor("wgt", [D, D], F32, kind="ExternalInput").ap(),
        wgate=nc.dram_tensor("wgate", [HID, D], I32, kind="ExternalInput").ap(),
        wup=nc.dram_tensor("wup", [HID, D], I32, kind="ExternalInput").ap(),
        wdown=nc.dram_tensor("wdown", [D, HID], I32, kind="ExternalInput").ap(),
    )
    gam_d = nc.dram_tensor("gam", [1, D], F32, kind="ExternalInput").ap()
    bet_d = nc.dram_tensor("bet", [1, D], F32, kind="ExternalInput").ap()
    ab_r_d = nc.dram_tensor("ab_r", [1, NH], F32, kind="ExternalInput").ap()
    ab_c_d = nc.dram_tensor("ab_c", [NH, 1], F32, kind="ExternalInput").ap()
    grid_d = nc.dram_tensor("gridc", [128, 128], F32, kind="ExternalInput").ap()
    mask_d = nc.dram_tensor("maskc", [128, 128], F32, kind="ExternalInput").ap()
    ones_d = nc.dram_tensor("onesc", [1, 128], F32, kind="ExternalInput").ap()
    shead_d = nc.dram_tensor("sheadc", [NH, D], F32, kind="ExternalInput").ap()
    ident_d = nc.dram_tensor("identc", [128, 128], BF16, kind="ExternalInput").ap()

    nat = {
        "wgate": nc.dram_tensor("nat_g", [HID, D], BF16, kind="Internal").ap(),
        "wup": nc.dram_tensor("nat_u", [HID, D], BF16, kind="Internal").ap(),
        "wdown": nc.dram_tensor("nat_d", [D, HID], BF16, kind="Internal").ap(),
    }
    for w in ("wq", "wv", "wo", "wgt"):
        nat[w] = nc.dram_tensor(f"nat_{w}", [D, D], BF16, kind="Internal").ap()
    scr_g = nc.dram_tensor("scr_g", [128, NSLAB, KC, SLAB], BF16, kind="Internal").ap()
    scr_u = nc.dram_tensor("scr_u", [128, NSLAB, KC, SLAB], BF16, kind="Internal").ap()
    scr_d = nc.dram_tensor("scr_d", [128, NSLAB, SLAB // 128, D], BF16, kind="Internal").ap()
    scr_a = {w: nc.dram_tensor(f"scr_{w}", [128, KC, D], BF16, kind="Internal").ap()
             for w in ("wq", "wv", "wo", "wgt")}

    es = ExitStack()
    sb = lambda n, sh, dt: es.enter_context(nc.sbuf_tensor(n, sh, dt))
    pst = lambda n, sh, dt: es.enter_context(nc.psum_tensor(n, sh, dt))

    X = sb("X", [128, NCH * D], BF16)
    XN = sb("XN", [128, KC * S], BF16)
    WA = sb("WA", [128, KC * SLAB], BF16)
    WB = sb("WB", [128, KC * SLAB], BF16)
    SG = sb("SG", [128, 4 * S], BF16)
    H = sb("H", [128, 8 * S], BF16)
    ACC = sb("ACC", [128, NCH * D], BF16)

    LBUF = sb("LBUF", [128, NH * 128], F32)
    LAMV = sb("LAMV", [128, D], F32)
    GREP = sb("GREP", [128, D], BF16)
    BREP = sb("BREP", [128, D], BF16)
    LLREP = sb("LLREP", [128, NH], F32)
    LAMR = sb("LAMR", [128, NH], F32)
    STATS = sb("STATS", [128, 128], F32)
    TMPTM = sb("TMPTM", [128, 2 * D], BF16)
    CROW = sb("CROW", [1, D], F32)
    TSTB = sb("TSTB", [128, 2 * D], BF16)
    CTMP = TSTB[0:1, :].bitcast(F32)
    IDN = sb("IDN", [128, 128], BF16)
    SM1 = sb("SM1", [1, NH], F32)
    SM2 = sb("SM2", [NH, 1], F32)
    SM3 = sb("SM3", [1, NH], F32)
    SM4 = sb("SM4", [NH, 1], F32)
    LVROW = CROW[0:1, :]

    PSB = [pst(f"PSB{i}", [128, 512], F32) for i in range(7)]
    PSB7 = pst("PSB7b", [128, 1024], BF16)

    ACCF = ACC[:].bitcast(F32)
    GRIDC = ACCF[:, 0:128]
    MASKC = ACCF[:, 128:256]
    ONESC = ACCF[0:1, 256:384]
    SHEADC = ACCF[0:NH, 384:1408]
    GAMR = ACCF[0:1, 1408:2432]
    BETR = ACCF[0:1, 2432:3456]

    HF = H[:].bitcast(F32)
    VSC = [HF[:, 1024 * i:1024 * (i + 1)] for i in range(2)]
    STSB = [HF[:, 1024 * (2 + i):1024 * (3 + i)] for i in range(2)]
    QC = [H[:, 8192 + 1024 * i:8192 + 1024 * (i + 1)] for i in range(2)]
    YC = [H[:, 10240 + 1024 * i:10240 + 1024 * (i + 1)] for i in range(2)]
    SGC = [H[:, 12288 + 1024 * i:12288 + 1024 * (i + 1)] for i in range(3)]
    WOT = ACC[:, 0:8192]
    TYC = [ACC[:, 8192 + 1024 * i:8192 + 1024 * (i + 1)] for i in range(2)] \
        + [ACC[:, 14336:15360]]
    Y2C = [ACC[:, 10240 + 1024 * i:10240 + 1024 * (i + 1)] for i in range(2)]
    Y2FM = [ACC[:, 12288 + 1024 * i:12288 + 1024 * (i + 1)] for i in range(2)]

    em = Em()
    em.stream("pc", "gpsimd")
    em.stream("cl", "gpsimd")
    for q in range(4):
        em.stream(f"io{q}", "gpsimd")
    em.stream("cr", "gpsimd")
    em.stream("ns", "sync")
    em.stream("ts", "sync")
    em.stream("wl", "sync")
    em.stream("tl", "sync")

    kX = [f"X{c}" for c in range(NCH)]
    kXN = [[f"XN{f}c{c}" for c in range(NCH)] for f in range(KC)]
    kSG = [[f"SG{m}g{g}" for g in range(4)] for m in range(4)]
    kH = [[f"H{hc}g{g}" for g in range(4)] for hc in range(8)]
    kACC = [[f"ACC{c}n{n}" for n in range(2)] for c in range(NCH)]
    kPSB = [f"PSB{i}" for i in range(8)]
    ALL_SG = [k for row in kSG for k in row]
    ALL_H = [k for row in kH for k in row]
    ALL_ACC = [k for row in kACC for k in row]
    kVSC = ["VSC0", "VSC1"]
    kSTSB = ["STSB0", "STSB1"]
    kQC = ["QC0", "QC1"]
    kYC = ["YC0", "YC1"]
    kSGC = ["SGC0", "SGC1", "SGC2"]
    kTYC = ["TYC0", "TYC1", "TYC2"]
    kY2C = ["Y2C0", "Y2C1"]
    kY2FM = ["Y2FM0", "Y2FM1"]
    B_H = kVSC + kSTSB + kQC + kYC + kSGC
    B_ACC = ["WOT"] + kTYC + kY2C + kY2FM
    PREPC_KEYS = ["PREPC", "PREPC2", "PREPC3", "PREPC4", "PREPC5", "PREPC6"]
    PREP_SG_KEYS = (["tst0", "tst1"]
                    + [f"tst{a}_{i}" for a in range(2) for i in range(3)])
    WAR_ACC_B = ALL_ACC
    WAR_ACC_A = B_ACC + PREPC_KEYS

    em.dma("cl", lambda g: g.dma_start(out=GRIDC, in_=grid_d), writes=["PREPC"])
    em.dma("cl", lambda g: g.dma_start(out=MASKC, in_=mask_d), writes=["PREPC2"])
    em.dma("cl", lambda g: g.dma_start(out=ONESC, in_=ones_d), writes=["PREPC3"])
    em.dma("cl", lambda g: g.dma_start(out=SHEADC, in_=shead_d), writes=["PREPC4"])
    em.dma("cl", lambda g: g.dma_start(out=GAMR, in_=gam_d), writes=["PREPC5"])
    em.dma("cl", lambda g: g.dma_start(out=BETR, in_=bet_d), writes=["PREPC6"])
    em.dma("cl", lambda g: g.dma_start(out=SM3[:], in_=ab_r_d), writes=["SM3"])
    em.dma("cl", lambda g: g.dma_start(out=SM4[:], in_=ab_c_d), writes=["SM4"])
    em.dma("cl", lambda g: g.dma_start(out=IDN[:], in_=ident_d), writes=["IDN"])

    def prep_cast(tag):
        em.dma("pc", lambda g, a=nat[tag], b=w_dram[tag]: g.dma_start(out=a, in_=b),
               writes=[f"nat_{tag}"])

    def prep_T_gu(tag, scr):
        if True:
            for kc in range(KC):
                slot = SG[:, (kc % 2) * HID:(kc % 2) * HID + HID]
                skey = f"tst{kc % 2}"
                em.dma("tl", lambda s, a=slot, b=nat[tag][:, 128 * kc:128 * (kc + 1)]:
                       s.dma_start(out=a, in_=b, transpose=True),
                       reads=[f"nat_{tag}"], writes=[skey],
                       war=ALL_SG + ["WGT"] + [f"tst{(kc % 2)}_{i}" for i in range(3)])
                dst = scr[:, :, kc, :]
                srcv = slot.rearrange("p (j h) -> p j h", h=SLAB)
                em.dma("ts", lambda s, a=dst, b=srcv: s.dma_start(out=a, in_=b),
                       reads=[skey], writes=[f"scrT_{tag}_{kc}"])
    def prep_T_d():
        for h0 in range(0, HID // 128, 3):
            half = (h0 // 3) % 2
            slot = SG[:, half * HID:half * HID + 3 * D]
            skey = f"tst{half}"
            for i in range(3):
                hc = h0 + i
                em.dma("tl", lambda s, a=slot[:, D * i:D * (i + 1)],
                       b=nat["wdown"][:, 128 * hc:128 * (hc + 1)]:
                       s.dma_start(out=a, in_=b, transpose=True),
                       reads=["nat_wdown"], writes=[f"{skey}_{i}"],
                       war=ALL_SG + ["WGT", skey])
            for i in range(3):
                hc = h0 + i
                dst = scr_d[:, hc // 8, hc % 8, :]
                em.dma("ts", lambda s, a=dst, b=slot[:, D * i:D * (i + 1)]:
                       s.dma_start(out=a, in_=b),
                       reads=[f"{skey}_{i}"], writes=[f"scrT_wdown_{hc}"])

    def prep_T_attn():
        for tag in ("wq", "wv", "wgt", "wo"):
            for kc in range(KC):
                slot = TSTB[:, (kc % 2) * D:(kc % 2) * D + D]
                skey = f"tstb{kc % 2}"
                em.dma("tl", lambda s, a=slot, b=nat[tag][:, 128 * kc:128 * (kc + 1)]:
                       s.dma_start(out=a, in_=b, transpose=True),
                       reads=[f"nat_{tag}"], writes=[skey])
                dst = scr_a[tag][:, kc, :]
                em.dma("ts", lambda s, a=dst, b=slot: s.dma_start(out=a, in_=b),
                       reads=[skey], writes=[f"scrT_{tag}_{kc}"])

    def build_consts():
        em.op("scalar", lambda s: s.activation(SM2[:], SM4[:], AF.Sigmoid),
              reads=["SM4"], writes=["SM2"])
        em.op("scalar", lambda s: s.activation(SM1[:], SM3[:], AF.Sigmoid),
              reads=["SM3"], writes=["SM1"])
        em.op("scalar", lambda s: s.activation(SM1[:], SM1[:], AF.Ln),
              reads=["SM1"], writes=["SM1"])
        em.group("tensor", [(lambda t: t.matmul(PSB[0][:, 0:NH], ONESC, SM1[:],
                                                start=True, stop=True),
                             ["SM1", "PREPC3"])], writes=["PSB0"])
        em.op("vector", lambda v: v.tensor_copy(LLREP[:], PSB[0][:, 0:NH]),
              reads=["PSB0"], writes=["LLREP"])
        for n in range(2):
            em.group("tensor", [(lambda t, n=n: t.matmul(
                PSB[1][0:1, :], SM2[:, 0:1], SHEADC[:, 512 * n:512 * (n + 1)],
                start=True, stop=True), ["SM2", "PREPC4"])], writes=["PSB1"])
            em.op("vector", lambda v, n=n: v.tensor_copy(
                LVROW[0:1, 512 * n:512 * (n + 1)], PSB[1][0:1, :]),
                reads=["PSB1"], writes=[f"LVROW{n}"])
        for n in range(2):
            em.group("tensor", [(lambda t, n=n: t.matmul(
                PSB[2][:, :], ONESC, LVROW[0:1, 512 * n:512 * (n + 1)],
                start=True, stop=True), [f"LVROW{n}", "PREPC3"])], writes=["PSB2"])
            em.op("vector", lambda v, n=n: v.tensor_copy(
                LAMV[:, 512 * n:512 * (n + 1)], PSB[2][:, :]),
                reads=["PSB2"], writes=["LAMV"])
        for src, dst, key, rk in ((GAMR, GREP, "GREP", "PREPC5"),
                                  (BETR, BREP, "BREP", "PREPC6")):
            for n in range(2):
                em.group("tensor", [(lambda t, src=src, n=n: t.matmul(
                    PSB[3][:, :], ONESC, src[:, 512 * n:512 * (n + 1)],
                    start=True, stop=True), [rk, "PREPC3"])], writes=["PSB3"])
                em.op("vector", lambda v, dst=dst, n=n: v.tensor_copy(
                    dst[:, 512 * n:512 * (n + 1)], PSB[3][:, :]),
                    reads=["PSB3"], writes=[key])
        for h in range(NH):
            em.op("scalar", lambda s, h=h: s.activation(
                LBUF[:, 128 * h:128 * (h + 1)], GRIDC, AF.Exp,
                scale=LLREP[:, h:h + 1]),
                reads=["LLREP", "PREPC"], writes=[f"LB{h}"])
            em.op("vector", lambda v, h=h: v.tensor_tensor(
                LBUF[:, 128 * h:128 * (h + 1)], LBUF[:, 128 * h:128 * (h + 1)],
                MASKC, ALU.mult),
                reads=[f"LB{h}", "PREPC2"], writes=[f"LB{h}"])
        for h in range(NH):
            em.op("scalar", lambda s, h=h: s.activation(
                LAMR[:, h:h + 1], GRIDC[:, 127:128], AF.Exp,
                scale=LLREP[:, h:h + 1], bias=LLREP[:, h:h + 1]),
                reads=["LLREP", "PREPC"], writes=["LAMR"])

    st_cols = dict(ssum=0, ssq=16, mu=32, musq=48, var=64, sd=80, rstd=96)

    def stat(namecol, c=None):
        base = st_cols[namecol]
        if c is None:
            return STATS[:, base:base + 16]
        return STATS[:, base + c:base + c + 1]

    MAGIC = 0x5f3759df

    def emit_rsqrt(varcol, varkey, wcol, wkey, tcol, tkey, zcol, zkey):
        em.op("vector", lambda v: v.tensor_scalar(wcol, varcol, float(EPS), None, ALU.add),
              reads=[varkey], writes=[wkey])
        wi, ti, zi = wcol.bitcast(I32), tcol.bitcast(I32), zcol.bitcast(I32)
        em.op("vector", lambda v: v.tensor_scalar(ti, wi, 1, None, ALU.arith_shift_right),
              reads=[wkey], writes=[tkey])
        em.op("vector", lambda v: v.tensor_scalar(zi, ti, -1, MAGIC, ALU.mult, ALU.add),
              reads=[tkey], writes=[zkey])
        for _ in range(3):
            em.op("vector", lambda v: v.tensor_tensor(tcol, zcol, zcol, ALU.mult),
                  reads=[zkey], writes=[tkey])
            em.op("vector", lambda v: v.tensor_tensor(tcol, tcol, wcol, ALU.mult),
                  reads=[tkey, wkey], writes=[tkey])
            em.op("vector", lambda v: v.tensor_scalar(tcol, tcol, -0.5, 1.5,
                                                      ALU.mult, ALU.add),
                  reads=[tkey], writes=[tkey])
            em.op("vector", lambda v: v.tensor_tensor(zcol, zcol, tcol, ALU.mult),
                  reads=[zkey, tkey], writes=[zkey])

    def ln_stats(c):
        xc = X[:, D * c:D * (c + 1)]
        em.op("vector", lambda v, xc=xc, c=c: v.reduce_sum(
            stat("ssum", c), xc, axis=AX.X), reads=[kX[c]], writes=["ST_ssum"])
        sq = TMPTM[:, D * (c % 2):D * (c % 2 + 1)]
        em.op("scalar", lambda s, xc=xc, c=c, sq=sq:
              s.activation(sq, xc, AF.Square, accum_out=stat("ssq", c)),
              reads=[kX[c]], writes=["ST_ssq", f"TT{c % 2}"])

    def layer_norm_to_xn(prestats=True):
        if not prestats:
            for c in range(NCH):
                ln_stats(c)
        em.op("vector", lambda v: v.tensor_scalar(
            stat("mu"), stat("ssum"), 1.0 / D, None, ALU.mult),
            reads=["ST_ssum"], writes=["ST_mu"])
        em.op("vector", lambda v: v.tensor_tensor(
            stat("musq"), stat("mu"), stat("mu"), ALU.mult),
            reads=["ST_mu"], writes=["ST_musq"])
        em.op("vector", lambda v: v.scalar_tensor_tensor(
            stat("var"), stat("ssq"), 1.0 / D, stat("musq"), ALU.mult, ALU.subtract),
            reads=["ST_ssq", "ST_musq"], writes=["ST_var"])
        emit_rsqrt(stat("var"), "ST_var", stat("sd"), "ST_sd",
                   stat("musq"), "ST_musq", stat("rstd"), "ST_rstd")
        for c in range(NCH):
            xc = X[:, D * c:D * (c + 1)]
            tm = TMPTM[:, D * (c % 2):D * (c % 2 + 1)]
            em.op("vector", lambda v, xc=xc, tm=tm, c=c: v.tensor_scalar(
                tm, xc, stat("mu", c), stat("rstd", c), ALU.subtract, ALU.mult),
                reads=[kX[c], "ST_mu", "ST_rstd"], writes=[f"TT{c % 2}"])
            bank, bkey = PSB7, "PSB7"
            items = [(lambda t, tm=tm, f=f, bank=bank: t.transpose(
                bank[:, 128 * f:128 * (f + 1)], tm[:, 128 * f:128 * (f + 1)], IDN[:]),
                [f"TT{c % 2}", "IDN"]) for f in range(KC)]
            em.group("tensor", items, writes=[bkey])
            xnv = XN[:].rearrange("p (f t) -> p f t", t=S)[:, :, 128 * c:128 * (c + 1)]
            bv = bank[:].rearrange("p (f t) -> p f t", t=128)
            em.op("scalar", lambda s, xnv=xnv, bv=bv: s.activation(xnv, bv, AF.Copy),
                  reads=[bkey], writes=[kXN[f][c] for f in range(KC)])

    def ffn_phase(tap=False, prestats=True, next_stats=True, tail_cb=None,
                  skip_gw0=False):
        layer_norm_to_xn(prestats=prestats)
        if tap:
            em.dma("st", lambda g: g.dma_start(out=dbg_xn, in_=XN[:]),
                   reads=[k for row in kXN for k in row], writes=["DBGXN"])
        nsl = NSLAB if DBG_SLABS is None else DBG_SLABS
        for j in range(nsl):
            li = 3 * j
            slots = [(WA, "WSLOTA"), (WB, "WSLOTB")]
            wg_sl, wg_k = slots[li % 2]
            wu_sl, wu_k = slots[(li + 1) % 2]
            wd_sl, wd_k = slots[(li + 2) % 2]
            if not (skip_gw0 and j == 0):
                em.dma("wl", lambda s, a=wg_sl, j=j: s.dma_start(
                    out=a[:].rearrange("p (k h) -> p k h", h=SLAB), in_=scr_g[:, j]),
                    reads=[f"scrT_wgate_{kc}" for kc in range(KC)], writes=[wg_k])
                em.dma("wl", lambda s, a=wu_sl, j=j: s.dma_start(
                    out=a[:].rearrange("p (k h) -> p k h", h=SLAB), in_=scr_u[:, j]),
                    reads=[f"scrT_wup_{kc}" for kc in range(KC)], writes=[wu_k])
            if tap and j == 0:
                em.dma("st", lambda g, a=wg_sl: g.dma_start(out=dbg_wa, in_=a[:]),
                       reads=[wg_k], writes=["DBGWA"])
            for hs in range(2):
                for g in range(4):
                    for m4 in range(4):
                        m = 4 * hs + m4
                        bank = (g * 4 + m4) % 4
                        items = [(lambda t, k=k, m=m, g=g, wg_sl=wg_sl, bank=bank:
                                  t.matmul(PSB[bank][:],
                                           wg_sl[:, SLAB * k + 128 * m:SLAB * k + 128 * (m + 1)],
                                           XN[:, S * k + 512 * g:S * k + 512 * (g + 1)],
                                           start=(k == 0), stop=(k == KC - 1)),
                                  [wg_k] + [kXN[k][4 * g + i] for i in range(4)])
                                 for k in range(KC)]
                        em.group("tensor", items, writes=[kPSB[bank]])
                        if tap and j == 0 and hs == 0 and g == 0 and m4 == 0:
                            dps = HF[:, 4096:4608]
                            em.op("vector", lambda v, bank=bank: v.tensor_copy(
                                dps, PSB[bank][:]),
                                reads=[kPSB[bank]],
                                writes=[kH[4][i] for i in range(4)], war=B_H)
                            em.dma("st", lambda g2: g2.dma_start(out=dbg_ps, in_=dps),
                                   reads=[kH[4][0]], writes=["DBGPS"])
                        dst = SG[:, S * m4 + 512 * g:S * m4 + 512 * (g + 1)]
                        em.op("scalar", lambda s, dst=dst, bank=bank:
                              s.activation(dst, PSB[bank][:], AF.Silu,
                                           scale=np32(1.0 / 127.0)),
                              reads=[kPSB[bank]], writes=[kSG[m4][g]],
                              war=["WGT"] + PREP_SG_KEYS)
                for g in range(4):
                    for m4 in range(4):
                        m = 4 * hs + m4
                        hc = m
                        bank = (g * 4 + m4) % 4
                        items = [(lambda t, k=k, m=m, g=g, wu_sl=wu_sl, bank=bank:
                                  t.matmul(PSB[bank][:],
                                           wu_sl[:, SLAB * k + 128 * m:SLAB * k + 128 * (m + 1)],
                                           XN[:, S * k + 512 * g:S * k + 512 * (g + 1)],
                                           start=(k == 0), stop=(k == KC - 1)),
                                  [wu_k] + [kXN[k][4 * g + i] for i in range(4)])
                                 for k in range(KC)]
                        em.group("tensor", items, writes=[kPSB[bank]])
                        dst = H[:, S * hc + 512 * g:S * hc + 512 * (g + 1)]
                        src0 = SG[:, S * m4 + 512 * g:S * m4 + 512 * (g + 1)]
                        em.op("vector", lambda v, dst=dst, src0=src0, bank=bank:
                              v.tensor_tensor(dst, src0, PSB[bank][:], ALU.mult),
                              reads=[kSG[m4][g], kPSB[bank]], writes=[kH[hc][g]],
                              war=B_H)
            if tap and j == 0:
                em.dma("st", lambda g: g.dma_start(out=dbg_sg, in_=SG[:]),
                       reads=ALL_SG, writes=["DBGSG"])
                em.dma("st", lambda g: g.dma_start(out=dbg_h, in_=H[:]),
                       reads=ALL_H, writes=["DBGH"])
            em.dma("wl", lambda s, a=wd_sl, j=j: s.dma_start(
                out=a[:].rearrange("p (k h) -> p k h", h=D), in_=scr_d[:, j]),
                reads=[f"scrT_wdown_{8 * j + i}" for i in range(8)], writes=[wd_k])
            for c in range(NCH):
                for n in range(2):
                    bank = 4 + (c * 2 + n) % 3
                    items = [(lambda t, hc=hc, c=c, n=n, wd_sl=wd_sl, bank=bank:
                              t.matmul(PSB[bank][:],
                                       H[:, S * hc + 128 * c:S * hc + 128 * (c + 1)],
                                       wd_sl[:, D * hc + 512 * n:D * hc + 512 * (n + 1)],
                                       start=(hc == 0), stop=(hc == 7)),
                              [wd_k, kH[hc][c // 4]])
                             for hc in range(8)]
                    em.group("tensor", items, writes=[kPSB[bank]])
                    dst = ACC[:, D * c + 512 * n:D * c + 512 * (n + 1)]
                    if j == 0:
                        em.op("scalar", lambda s, dst=dst, bank=bank:
                              s.activation(dst, PSB[bank][:], AF.Copy, scale=SFIN),
                              reads=[kPSB[bank]], writes=[kACC[c][n]], war=WAR_ACC_A)
                    else:
                        em.op("vector", lambda v, dst=dst, bank=bank:
                              v.scalar_tensor_tensor(dst, PSB[bank][:], SFIN, dst,
                                                     ALU.mult, ALU.add),
                              reads=[kPSB[bank], kACC[c][n]], writes=[kACC[c][n]])
                if j == NSLAB - 1 or (DBG_SLABS is not None and j == DBG_SLABS - 1):
                    xc = X[:, D * c:D * (c + 1)]
                    ac = ACC[:, D * c:D * (c + 1)]
                    em.op("vector", lambda v, xc=xc, ac=ac: v.tensor_tensor(
                        xc, xc, ac, ALU.add),
                        reads=[kX[c], kACC[c][0], kACC[c][1]], writes=[kX[c]])
                    if next_stats:
                        ln_stats(c)
                    if tail_cb is not None:
                        tail_cb(c)
        if tap:
            em.dma("st", lambda g: g.dma_start(out=dbg_acc, in_=ACC[:]),
                   reads=ALL_ACC, writes=["DBGACC"])

    def attn_phase():
        layer_norm_to_xn(prestats=True)
        em.dma("wl", lambda s: s.dma_start(
            out=WA[:].rearrange("p (k h) -> p k h", h=D), in_=scr_a["wq"]),
            reads=[f"scrT_wq_{kc}" for kc in range(KC)], writes=["WSLOTA"])
        em.dma("wl", lambda s: s.dma_start(
            out=WB[:].rearrange("p (k h) -> p k h", h=D), in_=scr_a["wv"]),
            reads=[f"scrT_wv_{kc}" for kc in range(KC)], writes=["WSLOTB"])
        em.dma("wl", lambda s: s.dma_start(
            out=SG[:].rearrange("p (k h) -> p k h", h=D), in_=scr_a["wgt"]),
            reads=[f"scrT_wgt_{kc}" for kc in range(KC)], writes=["WGT"],
            war=ALL_SG + PREP_SG_KEYS)
        em.dma("wl", lambda s: s.dma_start(
            out=WOT.rearrange("p (k h) -> p k h", h=D), in_=scr_a["wo"]),
            reads=[f"scrT_wo_{kc}" for kc in range(KC)], writes=["WOT"],
            war=[kACC[c][n] for c in range(8) for n in range(2)]
                + PREPC_KEYS)

        pslot = [0]

        def proj(c, wslot, wkey):
            banks = []
            for n in range(2):
                bank = 4 + pslot[0] % 3
                pslot[0] += 1
                banks.append(bank)
                items = [(lambda t, k=k, n=n, c=c, wslot=wslot, bank=bank:
                          t.matmul(PSB[bank][:],
                                   XN[:, S * k + 128 * c:S * k + 128 * (c + 1)],
                                   wslot[:, D * k + 512 * n:D * k + 512 * (n + 1)],
                                   start=(k == 0), stop=(k == KC - 1)),
                          [wkey, kXN[k][c]]) for k in range(KC)]
                em.group("tensor", items, writes=[kPSB[bank]])
            return banks

        def part1a(c):
            sl = c % 2
            s3 = c % 3
            vb = proj(c, WB, "WSLOTB")
            for n in range(2):
                em.op("scalar", lambda s, sl=sl, n=n, b=vb[n]: s.activation(
                    VSC[sl][:, 512 * n:512 * (n + 1)], PSB[b][:], AF.Copy),
                    reads=[kPSB[vb[n]]], writes=[kVSC[sl]], war=ALL_H)
            if c > 0:
                em.op("gpsimd", lambda g, sl=sl: g.tensor_tensor(
                    VSC[sl][0:1, :], VSC[sl][0:1, :], CROW[0:1, :], ALU.add),
                    reads=[kVSC[sl], "CROW"], writes=[kVSC[sl]])
            qb = proj(c, WA, "WSLOTA")
            for n in range(2):
                em.op("scalar", lambda s, sl=sl, n=n, b=qb[n]: s.activation(
                    QC[sl][:, 512 * n:512 * (n + 1)], PSB[b][:], AF.Copy),
                    reads=[kPSB[qb[n]]], writes=[kQC[sl]], war=ALL_H)
            gb = proj(c, SG, "WGT")
            for n in range(2):
                em.op("scalar", lambda s, s3=s3, n=n, b=gb[n]: s.activation(
                    SGC[s3][:, 512 * n:512 * (n + 1)], PSB[b][:], AF.Silu),
                    reads=[kPSB[gb[n]]], writes=[kSGC[s3]], war=ALL_H)
            sb0 = 2 * sl
            items = []
            for h in range(NH):
                bank = sb0 + (1 if h >= 8 else 0)
                off = 64 * h - 512 * (h // 8)
                items.append((lambda t, h=h, sl=sl, bank=bank, off=off:
                              t.matmul(PSB[bank][:, off:off + 64],
                                       LBUF[:, 128 * h:128 * (h + 1)],
                                       VSC[sl][:, 64 * h:64 * (h + 1)],
                                       start=True, stop=True),
                              [kVSC[sl], f"LB{h}"]))
            em.group("tensor", items, writes=[kPSB[sb0], kPSB[sb0 + 1]])

        def part1b(c):
            sl = c % 2
            s3 = c % 3
            sb0 = 2 * sl
            for n in range(2):
                em.op("vector", lambda v, sl=sl, n=n, sb0=sb0: v.tensor_tensor(
                    YC[sl][:, 512 * n:512 * (n + 1)], PSB[sb0 + n][:],
                    QC[sl][:, 512 * n:512 * (n + 1)], ALU.mult),
                    reads=[kPSB[sb0 + n], kQC[sl]], writes=[kYC[sl]],
                    war=ALL_H)
            yb = YC[sl]
            ysum = STATS[:, 112 + sl:113 + sl]
            ysq = STATS[:, 114 + sl:115 + sl]
            ymu = STATS[:, 116 + sl:117 + sl]
            yvar = STATS[:, 118 + sl:119 + sl]
            ysd = STATS[:, 120 + sl:121 + sl]
            yrs = STATS[:, 122 + sl:123 + sl]
            ymsq = STATS[:, 124 + sl:125 + sl]
            em.op("vector", lambda v, yb=yb, ysum=ysum: v.reduce_sum(
                ysum, yb, axis=AX.X), reads=[kYC[sl]], writes=[f"YSS{sl}"])
            sq = TMPTM[:, D * sl:D * (sl + 1)]
            em.op("scalar", lambda s, yb=yb, sq=sq, ysq=ysq:
                  s.activation(sq, yb, AF.Square, accum_out=ysq),
                  reads=[kYC[sl]], writes=[f"YSQ{sl}", f"TT{sl}"])
            em.op("vector", lambda v, ysum=ysum, ymu=ymu: v.tensor_scalar(
                ymu, ysum, 1.0 / D, None, ALU.mult),
                reads=[f"YSS{sl}"], writes=[f"YMU{sl}"])
            em.op("vector", lambda v, ymu=ymu, ymsq=ymsq: v.tensor_tensor(
                ymsq, ymu, ymu, ALU.mult), reads=[f"YMU{sl}"], writes=[f"YMSQ{sl}"])
            em.op("vector", lambda v, ysq=ysq, ymsq=ymsq, yvar=yvar:
                  v.scalar_tensor_tensor(yvar, ysq, 1.0 / D, ymsq,
                                         ALU.mult, ALU.subtract),
                  reads=[f"YSQ{sl}", f"YMSQ{sl}"], writes=[f"YVAR{sl}"])
            emit_rsqrt(yvar, f"YVAR{sl}", ysd, f"YSD{sl}",
                       ymsq, f"YMSQ{sl}", yrs, f"YRS{sl}")
            ty = TYC[s3]
            em.op("vector", lambda v, yb=yb, ty=ty, ymu=ymu, yrs=yrs:
                  v.tensor_scalar(ty, yb, ymu, yrs, ALU.subtract, ALU.mult),
                  reads=[kYC[sl], f"YMU{sl}", f"YRS{sl}"], writes=[kTYC[s3]],
                  war=WAR_ACC_B)
            em.op("vector", lambda v, ty=ty: v.tensor_tensor(ty, ty, GREP[:], ALU.mult),
                  reads=[kTYC[s3], "GREP"], writes=[kTYC[s3]])
            em.op("vector", lambda v, ty=ty: v.tensor_tensor(ty, ty, BREP[:], ALU.add),
                  reads=[kTYC[s3], "BREP"], writes=[kTYC[s3]])

        def carrymm(c):
            sl = c % 2
            pb0 = 2 * ((c + 1) % 2)
            items = []
            for h in range(NH):
                bank = pb0 + (1 if h >= 8 else 0)
                off = 64 * h - 512 * (h // 8)
                items.append((lambda t, h=h, sl=sl, bank=bank, off=off:
                              t.matmul(PSB[bank][0:1, off:off + 64],
                                       LAMR[:, h:h + 1],
                                       VSC[sl][:, 64 * h:64 * (h + 1)],
                                       start=True, stop=True),
                              [kVSC[sl], "LAMR"]))
            em.group("tensor", items, writes=[kPSB[pb0], kPSB[pb0 + 1]])
            for n in range(2):
                em.op("scalar", lambda s, n=n, pb0=pb0: s.activation(
                    CROW[0:1, 512 * n:512 * (n + 1)], PSB[pb0 + n][0:1, :], AF.Copy),
                    reads=[kPSB[pb0 + n]], writes=["CROW"],
                    war=["LVROW0", "LVROW1"])

        def part2(c):
            sl = c % 2
            s3 = c % 3
            ty = TYC[s3]
            y2 = Y2C[sl]
            em.op("vector", lambda v, y2=y2, ty=ty, s3=s3: v.tensor_tensor(
                y2, ty, SGC[s3], ALU.mult),
                reads=[kTYC[s3], kSGC[s3]], writes=[kY2C[sl]], war=WAR_ACC_B)
            bank, bkey = PSB7, "PSB7"
            items = [(lambda t, y2=y2, f=f, bank=bank: t.transpose(
                bank[:, 128 * f:128 * (f + 1)], y2[:, 128 * f:128 * (f + 1)], IDN[:]),
                [kY2C[sl], "IDN"]) for f in range(KC)]
            em.group("tensor", items, writes=[bkey])
            em.op("scalar", lambda s, sl=sl, bank=bank: s.activation(
                Y2FM[sl], bank[:], AF.Copy),
                reads=[bkey], writes=[kY2FM[sl]], war=WAR_ACC_B)
            for n in range(2):
                bank2 = 4 + pslot[0] % 3
                pslot[0] += 1
                items = [(lambda t, k=k, n=n, sl=sl, bank2=bank2:
                          t.matmul(PSB[bank2][:],
                                   Y2FM[sl][:, 128 * k:128 * (k + 1)],
                                   WOT[:, D * k + 512 * n:D * k + 512 * (n + 1)],
                                   start=(k == 0), stop=(k == KC - 1)),
                          ["WOT", kY2FM[sl]]) for k in range(KC)]
                em.group("tensor", items, writes=[kPSB[bank2]])
                xc = X[:, D * c + 512 * n:D * c + 512 * (n + 1)]
                em.op("vector", lambda v, xc=xc, bank2=bank2: v.tensor_tensor(
                    xc, xc, PSB[bank2][:], ALU.add),
                    reads=[kX[c], kPSB[bank2]], writes=[kX[c]])
            ln_stats(c)

        for c in range(NCH + 2):
            if c < NCH:
                part1a(c)
            if c >= 1:
                part1b(c - 1)
            if c < NCH - 1:
                carrymm(c)
            if c >= 2:
                part2(c - 2)

    def load_q(si, q):
        src = x_d[si][512 * q:512 * (q + 1), :].rearrange("(c p) d -> p c d", p=128)
        dstv = X[:, 4 * D * q:4 * D * (q + 1)].rearrange("p (c d) -> p c d", d=D)
        em.dma(f"io{q}", lambda g, a=dstv, b=src: g.dma_start(out=a, in_=b),
               writes=[kX[4 * q + i] for i in range(4)])

    load_q(0, 0)
    prep_cast("wgate")
    prep_cast("wup")
    prep_cast("wdown")
    for q in range(1, 4):
        load_q(0, q)
    for w in ("wq", "wv", "wgt", "wo"):
        prep_cast(w)
    build_consts()
    prep_T_gu("wgate", scr_g)
    em.dma("wl", lambda s: s.dma_start(
        out=WA[:].rearrange("p (k h) -> p k h", h=SLAB), in_=scr_g[:, 0]),
        reads=[f"scrT_wgate_{kc}" for kc in range(KC)], writes=["WSLOTA"])
    prep_T_gu("wup", scr_u)
    em.dma("wl", lambda s: s.dma_start(
        out=WB[:].rearrange("p (k h) -> p k h", h=SLAB), in_=scr_u[:, 0]),
        reads=[f"scrT_wup_{kc}" for kc in range(KC)], writes=["WSLOTB"])
    prep_T_d()
    for s in range(SEQS):
        if "A" in PHASES:
            ffn_phase(tap=(DBG_TAPS and s == 0), prestats=(s > 0),
                      next_stats=("B" in PHASES or "C" in PHASES),
                      skip_gw0=(s == 0))
        if s == 0:
            prep_T_attn()
        if "B" in PHASES:
            attn_phase()
        def boundary_cb(c, s=s):
            if c % 4 != 3:
                return
            q = c // 4
            dso = o_d[s][512 * q:512 * (q + 1), :].rearrange("(c p) d -> p c d", p=128)
            srcv = X[:, 4 * D * q:4 * D * (q + 1)].rearrange("p (c d) -> p c d", d=D)
            em.dma(f"io{q}", lambda g, a=dso, b=srcv: g.dma_start(out=a, in_=b),
                   reads=[kX[4 * q + i] for i in range(4)], writes=[f"OUTD{q}"])
            if s + 1 < SEQS:
                src = x_d[s + 1][512 * q:512 * (q + 1), :].rearrange("(c p) d -> p c d", p=128)
                dstv = X[:, 4 * D * q:4 * D * (q + 1)].rearrange("p (c d) -> p c d", d=D)
                em.dma(f"io{q}", lambda g, a=dstv, b=src: g.dma_start(out=a, in_=b),
                       writes=[kX[4 * q + i] for i in range(4)])
                if q >= 1:
                    for i in range(4):
                        ln_stats(4 * (q - 1) + i)
        if "C" in PHASES:
            ffn_phase(prestats=("B" in PHASES), next_stats=False,
                      tail_cb=boundary_cb)
            if s + 1 < SEQS:
                for i in range(4):
                    ln_stats(12 + i)
        else:
            for c in range(NCH):
                boundary_cb(c)
            if s + 1 < SEQS:
                for i in range(4):
                    ln_stats(12 + i)

    em.op("gpsimd", lambda g: g.nop(),
          deps_extra=[("d", f"io{q}", 2 * SEQS) for q in range(4)])

    em.finalize_and_emit(nc, es)
    es.close()
    return nc


def _consts():
    tt = np.arange(128)
    g = (tt[None, :] - tt[:, None]).astype(np.float32)
    msk = (g >= 0).astype(np.float32)
    g = np.where(g >= 0, g, -1.0).astype(np.float32)
    shead = np.zeros((NH, D), np.float32)
    for h in range(NH):
        shead[h, HD * h:HD * (h + 1)] = 1.0
    return dict(gridc=g, maskc=msk, onesc=np.ones((1, 128), np.float32),
                sheadc=shead, identc=np.eye(128, dtype=ml_dtypes.bfloat16))


def kernel(L0_attn_x, L0_attn_Wq, L0_attn_Wv, L0_attn_Wo, L0_attn_Wg,
           L0_attn_beta, L0_attn_ln_gamma, L0_attn_ln_beta,
           L0_ffn_w_gate, L0_ffn_w_up, L0_ffn_w_down):
    x = np.ascontiguousarray(np.asarray(L0_attn_x, dtype=np.float32))
    shared = dict(
        wq=np.ascontiguousarray(L0_attn_Wq, dtype=np.float32),
        wv=np.ascontiguousarray(L0_attn_Wv, dtype=np.float32),
        wo=np.ascontiguousarray(L0_attn_Wo, dtype=np.float32),
        wgt=np.ascontiguousarray(L0_attn_Wg, dtype=np.float32),
        wgate=np.ascontiguousarray(L0_ffn_w_gate, dtype=np.int32),
        wup=np.ascontiguousarray(L0_ffn_w_up, dtype=np.int32),
        wdown=np.ascontiguousarray(L0_ffn_w_down, dtype=np.int32),
        gam=np.ascontiguousarray(np.asarray(L0_attn_ln_gamma, np.float32).reshape(1, D)),
        bet=np.ascontiguousarray(np.asarray(L0_attn_ln_beta, np.float32).reshape(1, D)),
        ab_r=np.ascontiguousarray(np.asarray(L0_attn_beta, np.float32).reshape(1, NH)),
        ab_c=np.ascontiguousarray(np.asarray(L0_attn_beta, np.float32).reshape(NH, 1)),
        **_consts(),
    )
    nc = build_program()
    in_maps = []
    for c in range(NCORES):
        m = dict(shared)
        m["x"] = np.ascontiguousarray(x[SEQS * c:SEQS * (c + 1)])
        in_maps.append(m)
    res = run_bass_kernel_spmd(nc, in_maps, core_ids=list(range(NCORES)))
    if DBG_TAPS:
        global DBG_RESULT
        DBG_RESULT = res.results[0]
    return np.concatenate([r["o"] for r in res.results], axis=0)
